# version 12
# baseline (speedup 1.0000x reference)
import contextlib
import numpy as np
import ml_dtypes
import concourse.bass as bass
import concourse.mybir as mybir
from concourse.bass_utils import run_bass_kernel_spmd

F32 = mybir.dt.float32
BF16 = mybir.dt.bfloat16
U8 = mybir.dt.uint8
F16 = mybir.dt.float16
AF = mybir.ActivationFunctionType
ALU = mybir.AluOpType
BF = ml_dtypes.bfloat16

H = 2048
L = 2048
DI = 4096
NST = 16
RDT = 128
KCV = 4
IFF = 5632
NHEAD = 16
DH = 128
NEG = -30000.0
NHT = H // 128
NDT = DI // 128
NIT = IFF // 128
LNEPS = 1e-5
RMSEPS = 1e-6
SCAT = float(DH) ** -0.25
CFCV = [-2.0, -6.0, 2.0, -1.0, 1.0, LNEPS, RMSEPS, 1.0 / H, -0.5, SCAT, 0.0]


def _pi(n):
    return np.concatenate([np.arange(0, n, 2), np.arange(1, n, 2)])


class Sem:
    def __init__(self, ctx, nc, name):
        self.h = ctx.enter_context(nc.semaphore(name))
        self.name = name
        self.n = 0
        self.maxw = 0

    def inc(self, instr, k=1):
        instr.then_inc(self.h, k)
        self.n += k
        return instr

    def dma(self, eng, out, in_):
        self.wait(eng, self.n)
        return self.inc(eng.dma_start(out, in_), 16)

    def dman(self, eng, out, in_):
        return self.inc(eng.dma_start(out, in_), 16)

    def wait(self, eng, val=None):
        if val is None:
            val = self.n
        if val <= 0:
            return
        self.maxw = max(self.maxw, val)
        eng.wait_ge(self.h, val)


class K:
    def __init__(self, nc, blk, ctx):
        self.nc, self.blk, self.ctx = nc, blk, ctx
        self.sems = {}
        self.entry = []

    def sem(self, name):
        if name not in self.sems:
            self.sems[name] = Sem(self.ctx, self.nc, name)
        return self.sems[name]

    def on(self, eng, fn):
        getattr(self.blk, eng)(fn)

    def phase_entry(self, engines=("sync", "gpsimd", "tensor", "scalar", "vector")):
        gate = [(s, s.n) for s in self.entry if s.n > 0]
        for e in engines:
            def f(eng, gate=gate):
                for s, v in gate:
                    s.wait(eng, v)
            self.on(e, f)

    def check(self):
        for s in self.sems.values():
            assert s.maxw <= s.n, f"sem {s.name}: wait {s.maxw} > total {s.n}"


def xap(t, row0, l0, lw, stride=L):
    return bass.AP(t, row0 * stride + l0, [[stride, 128], [1, lw]])


def build_program(upto=6, taps=()):
    nc = bass.Bass("TRN2", target_bir_lowering=False, debug=False)
    nc.detect_race_conditions = False

    def dram(name, shape, dt, kind="ExternalInput"):
        return nc.dram_tensor(name, shape, dt, kind=kind)

    def scratch(name, shape, dt):
        kind = "ExternalOutput" if name in taps else "Internal"
        return nc.dram_tensor(name, shape, dt, kind=kind)

    x_t = dram("x_t", [H, L], F32)
    w_in_s = dram("w_in_s", [H, DI], F32)
    w_in_g = dram("w_in_g", [H, DI], F32)
    wdt_r = dram("wdt_r", [128, NDT * RDT], F32)
    wbc_r = dram("wbc_r", [128, NDT * 2 * NST], F32)
    w_dtp_T = dram("w_dtp_T", [RDT, DI], F32)
    w_out_T = dram("w_out_T", [DI, H], F32)
    conv_w_r = dram("conv_w_r", [128, NDT * KCV], F32)
    conv_b_r = dram("conv_b_r", [128, NDT], F32)
    b_dtp_r = dram("b_dtp_r", [128, NDT], F32)
    a_log_r = dram("a_log_r", [128, NDT * NST], F32)
    d_r = dram("d_r", [128, NDT], F32)
    wq_T = dram("wq_T", [H, H], F32)
    wk_T = dram("wk_T", [H, H], F32)
    wv_T = dram("wv_T", [H, H], F32)
    wo_T = dram("wo_T", [H, H], F32)
    bq_r = dram("bq_r", [128, NHT], F32)
    bk_r = dram("bk_r", [128, NHT], F32)
    bv_r = dram("bv_r", [128, NHT], F32)
    bo_r = dram("bo_r", [128, NHT], F32)
    gub = dram("gub", [H // 2, 2 * IFF], U8)
    dnb = dram("dnb", [IFF // 2, H], U8)
    gu_scl = dram("gu_scl", [NST, 2 * IFF], F16)
    dn_scl = dram("dn_scl", [NIT, H], F16)
    e_gu = dram("e_gu", [NST, 8 * 128], F16)
    e_dn = dram("e_dn", [NIT, 22 * 128], F16)
    ones1 = dram("ones1", [1, 128], BF16)
    onesc = dram("onesc", [128, 1], BF16)
    ident = dram("ident", [128, 128], BF16)
    identf = dram("identf", [128, 128], F32)
    e_sel = dram("e_sel", [NST, NST * 128], BF16)
    mdiag = dram("mdiag", [128, 4 * 512], BF16)
    mdiag1 = dram("mdiag1", [128, 128], F32)
    cfc = dram("cfc", [128, 16], F32)

    out_t = dram("out_t", [H, L], F32, kind="ExternalOutput")

    u_s = scratch("u_s", [DI, L], BF16)
    sg_s = scratch("sg_s", [DI, L], BF16)
    y2_s = scratch("y2_s", [DI, L], BF16)
    bc_s = scratch("bc_s", [2 * NST, L], BF16)
    ts_s = scratch("ts_s", [RDT, L], BF16)
    ypart_s = scratch("ypart_s", [DI, L], F32)
    aout_s = scratch("aout_s", [H, L], BF16)
    xr1 = scratch("xr1", [H, L], F32)
    xr2 = scratch("xr2", [H, L], F32)
    xr3 = scratch("xr3", [H, L], F32)
    wgu_s = scratch("wgu_s", [H, 2 * IFF], BF16)
    wdn_s = scratch("wdn_s", [IFF, H], BF16)

    with contextlib.ExitStack() as ctx:
        blk = ctx.enter_context(nc.Block())
        k = K(nc, blk, ctx)
        cE = ctx.enter_context
        cst = {
            "ones1": cE(nc.sbuf_tensor("c_ones1", [1, 128], BF16)),
            "onesc": cE(nc.sbuf_tensor("c_onesc", [128, 1], BF16)),
            "ident": cE(nc.sbuf_tensor("c_ident", [128, 128], BF16)),
            "identf": cE(nc.sbuf_tensor("c_identf", [128, 128], F32)),
            "cfc": cE(nc.sbuf_tensor("c_cfc", [128, 16], F32)),
        }
        cfcv = cst["cfc"]
        for ci, v in enumerate(CFCV):
            nc.const_aps.register(F32, float(v), cfcv[:, ci:ci + 1])
        sld = k.sem("dq_in")

        def load_consts(sync):
            for key, src in [("ones1", ones1), ("onesc", onesc),
                             ("ident", ident), ("identf", identf),
                             ("cfc", cfc)]:
                sld.dma(sync, cst[key][:], src.ap())
        k.on("sync", load_consts)
        k.entry = [sld]

        if upto == 0:
            phase_dequant(k, cst, gub, dnb, gu_scl, dn_scl, e_gu, e_dn,
                          wgu_s, wdn_s)
        else:
            phase_mamba_a(k, cst, x_t, w_in_s, w_in_g, wdt_r, wbc_r,
                          conv_w_r, conv_b_r, u_s, sg_s, bc_s, ts_s,
                          dq_args=(gub, dnb, gu_scl, dn_scl, e_gu, e_dn,
                                   wgu_s, wdn_s))
        if upto >= 2:
            phase_ssm(k, cst, u_s, sg_s, bc_s, ts_s, w_dtp_T, b_dtp_r,
                      a_log_r, d_r, ypart_s, y2_s)
        if upto >= 3:
            phase_proj_res(k, cst, "p3", w_out_T, DI, y2_s, x_t,
                           xr1 if upto > 3 else out_t, bias_r=None)
        if upto >= 4:
            phase_ffn(k, cst, "f1", xr1, wgu_s, wdn_s,
                      xr2 if upto > 4 else out_t)
        if upto >= 5:
            phase_attn(k, cst, xr2, wq_T, wk_T, wv_T, bq_r, bk_r, bv_r,
                       e_sel, mdiag, mdiag1, aout_s)
            phase_proj_res(k, cst, "wo", wo_T, H, aout_s, xr2,
                           xr3 if upto > 5 else out_t, bias_r=bo_r)
        if upto >= 6:
            phase_ffn(k, cst, "f2", xr3, wgu_s, wdn_s, out_t)

        fin = [(s, s.n) for s in k.sems.values() if s.n > 0]

        def final_wait(g, fin=fin):
            for s, v in fin:
                s.wait(g, v)
        k.on("gpsimd", final_wait)
        k.check()
    return nc


def phase_dequant(k, cst, gub, dnb, gu_scl, dn_scl, e_gu, e_dn, wgu_s, wdn_s,
                  defer=False, E=None, g_pre=None, g_hook=None):
    nc = k.nc
    sDQI = k.sem("dq_i")
    sDQS = k.sem("dq_s")
    sV = k.sem("dq_v")

    OC = 1024
    units = [("gu", tb, c) for tb in range(8) for c in range(11)] + \
            [("dn", tb, c) for tb in range(22) for c in range(2)]

    def uinfo(u):
        which, tb, c = u
        if which == "gu":
            return which, tb, c * 1024, 1024, 2 * IFF, gub, wgu_s, H // 2
        return which, tb, c * 1024, 1024, H, dnb, wdn_s, IFF // 2

    ctx = None
    if E is None:
        ctx = contextlib.ExitStack()
        E = ctx.enter_context
    byt = [E(nc.sbuf_tensor(f"dq_byt{i}", [128, OC], U8)) for i in range(4)]
    scl = [E(nc.sbuf_tensor(f"dq_scl{i}", [128, OC], BF16)) for i in range(4)]
    xb = E(nc.sbuf_tensor("dq_x", [128, OC], U8))
    s8 = E(nc.sbuf_tensor("dq_s8", [128, OC], U8))
    a1 = E(nc.sbuf_tensor("dq_a1", [128, OC], BF16))
    sgn = E(nc.sbuf_tensor("dq_sgn", [128, OC], BF16))
    r1 = E(nc.sbuf_tensor("dq_r1", [128, OC], BF16))
    r2 = E(nc.sbuf_tensor("dq_r2", [128, OC], BF16))
    wv = [E(nc.sbuf_tensor(f"dq_wv{i}", [128, OC], BF16)) for i in range(2)]

    def in_load(g, ui):
        if ui >= len(units):
            return
        which, tb, o0, oc, O, src, outd, hr = uinfo(units[ui])
        sscl = gu_scl if which == "gu" else dn_scl
        SW = 2 * IFF if which == "gu" else H
        if ui >= 4:
            sV.wait(g, 4 * (ui - 4) + 3)
        sDQI.dman(g, byt[ui % 4][:, 0:oc],
                 bass.AP(src, tb * 128 * O + o0, [[O, 128], [1, oc]]))
        if ui >= 4:
            sV.wait(g, 4 * (ui - 4) + 4)
        sDQI.dman(g, scl[ui % 4][0:64, 0:oc],
                 bass.AP(sscl, (2 * tb) * SW + o0, [[0, 64], [1, oc]]))
        sDQI.dman(g, scl[ui % 4][64:128, 0:oc],
                 bass.AP(sscl, (2 * tb + 1) * SW + o0, [[0, 64], [1, oc]]))

    def g_io(g):
        if g_pre is not None:
            g_pre(g)
        for i in range(4):
            in_load(g, i)
        for ui, u in enumerate(units):
            if g_hook is not None:
                g_hook(g, ui)
            which, tb, o0, oc, O, src, outd, hr = uinfo(u)
            sV.wait(g, 4 * ui + 2)
            sDQS.dman(g, 
                bass.AP(outd, tb * 128 * O + o0, [[O, 128], [1, oc]]),
                wv[0][:, 0:oc])
            in_load(g, ui + 4)
            sV.wait(g, 4 * ui + 4)
            sDQS.dman(g, 
                bass.AP(outd, (hr + tb * 128) * O + o0, [[O, 128], [1, oc]]),
                wv[1][:, 0:oc])

    def act_unit(scalar, ui):
        pass

    def dve_unit(vector, ui):
        if ui >= len(units):
            return
        which, tb, o0, oc, O, src, outd, hr = uinfo(units[ui])
        sDQI.wait(vector, 48 * ui + 48)
        b = byt[ui % 4][:, 0:oc]
        for p in range(2):
            si = 2 * ui + p
            if si >= 2:
                sDQS.wait(vector, 16 * (si - 1))
            wvb = wv[p][:, 0:oc]
            if p == 0:
                vector.tensor_single_scalar(xb[:, 0:oc], b, 7,
                                            op=ALU.bitwise_and)
                vector.tensor_single_scalar(s8[:, 0:oc], b, 8,
                                            op=ALU.bitwise_and)
            else:
                vector.tensor_scalar(xb[:, 0:oc], b, 4, 7,
                                     op0=ALU.logical_shift_right,
                                     op1=ALU.bitwise_and)
                vector.tensor_scalar(s8[:, 0:oc], b, 4, 8,
                                     op0=ALU.logical_shift_right,
                                     op1=ALU.bitwise_and)
            vector.tensor_scalar_mul(a1[:, 0:oc], xb[:, 0:oc], 0.5)
            sV.inc(vector.tensor_scalar(sgn[:, 0:oc], s8[:, 0:oc],
                                        -0.25, 1.0,
                                        op0=ALU.mult, op1=ALU.add), 1)
            vector.tensor_scalar(r1[:, 0:oc], a1[:, 0:oc], -2.0, 0.0,
                                 op0=ALU.add, op1=ALU.max)
            vector.tensor_scalar(r2[:, 0:oc], a1[:, 0:oc], -3.0, 0.0,
                                 op0=ALU.add, op1=ALU.max)
            vector.tensor_add(wvb, a1[:, 0:oc], r1[:, 0:oc])
            vector.scalar_tensor_tensor(wvb, r2[:, 0:oc],
                                        cst["cfc"][:, 2:3], wvb,
                                        op0=ALU.mult, op1=ALU.add)
            vector.tensor_mul(wvb, wvb, sgn[:, 0:oc])
            sV.inc(vector.tensor_mul(wvb, wvb,
                                     scl[ui % 4][:, 0:oc]), 1)

    if defer:
        k.on("gpsimd", g_io)
        return dict(dve=dve_unit, act=act_unit, n=len(units), sDQS=sDQS)

    with ctx:
        k.phase_entry()
        k.on("gpsimd", g_io)

        def v_main(vector):
            for ui in range(len(units)):
                dve_unit(vector, ui)
        k.on("vector", v_main)

        k.entry = [sDQS]


def emit_ln(k, cst, tag, x_src, l0, lw, lnb, bufs, base):
    nc = k.nc
    sX, sSTT, sROW, sBC, sLN = (k.sem(tag + s) for s in
                                ("_x", "_stat", "_row", "_bc", "_ln"))
    xin, xsq, rows, rbf, sB, msB, pstatB, pbc = bufs
    rwA, rwB, rwC = rows
    rbA, rbB = rbf
    pstatA = pbc
    X0, ST0, R0, BC0, LN0 = base
    ones1, onesc = cst["ones1"], cst["onesc"]
    nj = lw // 512

    def s_xload(sync):
        for i in range(NHT):
            if i >= 2:
                sSTT.wait(sync, ST0 + i - 1)
            sX.dma(sync, xin[i % 2][:, 0:lw],
                                  xap(x_src, i * 128, l0, lw))
    k.on("sync", s_xload)

    def s_sq(scalar):
        for i in range(NHT):
            sX.wait(scalar, X0 + 16 * (i + 1))
            if i >= 2:
                sSTT.wait(scalar, ST0 + i - 1)
            sROW.inc(scalar.activation(xsq[i % 2][:, 0:lw], xin[i % 2][:, 0:lw],
                                       AF.Square), 1)
    k.on("scalar", s_sq)

    def v_xbf(vector):
        for i in range(NHT):
            sX.wait(vector, X0 + 16 * (i + 1))
            sLN.inc(vector.tensor_copy(lnb[:, i * lw:(i + 1) * lw],
                                       xin[i % 2][:, 0:lw]), 1)
    k.on("vector", v_xbf)

    def t_stats(tensor):
        for i in range(NHT):
            sLN.wait(tensor, LN0 + i + 1)
            sROW.wait(tensor, R0 + i + 1)
            la = None
            for j in range(nj):
                tensor.matmul(pstatA[0:1, j * 512:(j + 1) * 512], onesc[:],
                              lnb[:, i * lw + j * 512:i * lw + (j + 1) * 512],
                              start=(i == 0), stop=(i == NHT - 1))
            for j in range(nj):
                la = tensor.matmul(pstatB[0:1, j * 512:(j + 1) * 512], onesc[:],
                                   xsq[i % 2][:, j * 512:(j + 1) * 512],
                                   start=(i == 0), stop=(i == NHT - 1))
            sSTT.inc(la, 1)
    k.on("tensor", t_stats)

    def s_rows(scalar):
        sSTT.wait(scalar, ST0 + NHT)
        scalar.copy(rwA, pstatA[0:1, 0:lw])
        sROW.inc(scalar.copy(rwB, pstatB[0:1, 0:lw]), 1)
    k.on("scalar", s_rows)
    RR = R0 + NHT + 1

    def v_rows(vector):
        sROW.wait(vector, RR)
        vector.tensor_scalar_mul(rwA, rwA, 1.0 / H)
        vector.tensor_mul(rwC, rwA, rwA)
        sROW.inc(vector.scalar_tensor_tensor(rwB, rwB, 1.0 / H, rwC,
                                             op0=ALU.mult, op1=ALU.subtract), 1)
    k.on("vector", v_rows)

    def s_rstd(scalar):
        sROW.wait(scalar, RR + 1)
        scalar.activation(rwC, rwB, AF.Ln, bias=LNEPS)
        sROW.inc(scalar.activation(rwB, rwC, AF.Exp, scale=-0.5), 1)
    k.on("scalar", s_rstd)

    def v_rows2(vector):
        sROW.wait(vector, RR + 2)
        vector.tensor_mul(rwC, rwA, rwB)
        vector.tensor_copy(rbA, rwB)
        sROW.inc(vector.tensor_copy(rbB, rwC), 1)
    k.on("vector", v_rows2)

    def t_bc(tensor):
        sROW.wait(tensor, RR + 3)
        la = None
        for j in range(nj):
            la = tensor.matmul(pbc[:, j * 512:(j + 1) * 512], ones1[:],
                               rbA[0:1, j * 512:(j + 1) * 512],
                               start=True, stop=True)
        sSTT.inc(la, 1)
        sBC.wait(tensor, BC0 + 1)
        for j in range(nj):
            la = tensor.matmul(pbc[:, j * 512:(j + 1) * 512], ones1[:],
                               rbB[0:1, j * 512:(j + 1) * 512],
                               start=True, stop=True)
        sSTT.inc(la, 1)
    k.on("tensor", t_bc)

    def s_bcev(scalar):
        sSTT.wait(scalar, ST0 + NHT + 1)
        sBC.inc(scalar.copy(sB[:, 0:lw], pbc[:, 0:lw]), 1)
        sSTT.wait(scalar, ST0 + NHT + 2)
        sBC.inc(scalar.copy(msB[:, 0:lw], pbc[:, 0:lw]), 1)
    k.on("scalar", s_bcev)

    def v_ln(vector):
        sBC.wait(vector, BC0 + 2)
        for i in range(NHT):
            t = lnb[:, i * lw:(i + 1) * lw]
            vector.tensor_mul(t, t, sB[:, 0:lw])
            sLN.inc(vector.tensor_sub(t, t, msB[:, 0:lw]), 1)
    k.on("vector", v_ln)


def phase_mamba_a(k, cst, x_t, w_in_s, w_in_g, wdt_r, wbc_r,
                  conv_w_r, conv_b_r, u_s, sg_s, bc_s, ts_s, dq_args=None):
    nc = k.nc
    sLN = k.sem("a_ln")
    sPRE = k.sem("a_pre")
    sW = k.sem("a_w")
    sWC = k.sem("a_wc")
    sPC = k.sem("a_pc")
    sMM = k.sem("a_mm")
    sHS = k.sem("a_hs")
    sSG = k.sem("a_sg")
    sCV = k.sem("a_cv")
    sSIL = k.sem("a_sil")
    sTS = k.sem("a_ts")
    sOUT = k.sem("a_out")

    with contextlib.ExitStack() as ctx:
        E = ctx.enter_context
        ln = E(nc.sbuf_tensor("a_lnf", [128, NHT * L], BF16))
        hsp = E(nc.sbuf_tensor("a_hsp", [128, L + KCV - 1], F32))
        acc = E(nc.sbuf_tensor("a_acc", [128, L], F32))
        u_t = [E(nc.sbuf_tensor(f"a_u{i}", [128, L], BF16)) for i in range(2)]
        sg_t = [E(nc.sbuf_tensor(f"a_sg{i}", [128, L], BF16)) for i in range(2)]
        wsA = [E(nc.sbuf_tensor(f"a_wsA{i}", [128, NHT * 128], BF16)) for i in range(2)]
        wsG = [E(nc.sbuf_tensor(f"a_wsG{i}", [128, NHT * 128], BF16)) for i in range(2)]
        cw = E(nc.sbuf_tensor("a_cw", [128, NDT * KCV], F32))
        cb = E(nc.sbuf_tensor("a_cb", [128, NDT], F32))
        wdt = E(nc.sbuf_tensor("a_wdt", [128, NDT * 128], BF16))
        wbc = E(nc.sbuf_tensor("a_wbc", [128, NDT * 32], BF16))
        ts_sb = E(nc.sbuf_tensor("a_ts", [128, L], F32))
        ts_bf = E(nc.sbuf_tensor("a_tsbf", [128, L], BF16))
        bc_sb = E(nc.sbuf_tensor("a_bc", [32, L], F32))
        bcbf = E(nc.sbuf_tensor("a_bcbf", [32, L], BF16))
        k.phase_entry()
        assert dq_args is not None

        def g_strip(g, c):
            if c >= 2 * NDT:
                return
            t, j = c // 2, c % 2
            wsrc = w_in_s if j == 0 else w_in_g
            dstt = wsA[t % 2] if j == 0 else wsG[t % 2]
            if t >= 2:
                sMM.wait(g, 12 * (t - 2) + 8)
            sW.dman(g, dstt[:].rearrange("p (t o) -> p t o", t=NHT),
                    bass.AP(wsrc, t * 128,
                            [[DI, 128], [128 * DI, NHT], [1, 128]]))

        def g_pre(g):
            sW.dman(g, wdt[:], wdt_r.ap())
            sW.dman(g, wbc[:], wbc_r.ap())
            for c in range(4):
                g_strip(g, c)

        DQN = 8 * 11 + 22 * 2
        ui_of_t = [DQN * t // NDT for t in range(NDT)]

        def g_hook(g, ui):
            for t in range(NDT):
                if ui_of_t[t] == ui and (t == 0 or ui_of_t[t - 1] != ui):
                    g_strip(g, 2 * t + 4)
                    g_strip(g, 2 * t + 5)

        dq = phase_dequant(k, cst, *dq_args, defer=True, E=E,
                           g_pre=g_pre, g_hook=g_hook)
        assert dq["n"] == DQN

        def dqu(t):
            return range(dq["n"] * t // NDT, dq["n"] * (t + 1) // NDT)

        with contextlib.ExitStack() as lctx:
            lE = lctx.enter_context
            xin = [lE(nc.sbuf_tensor(f"a_xin{i}", [128, L], F32)) for i in range(2)]
            xsq = [lE(nc.sbuf_tensor(f"a_xsq{i}", [128, L], BF16)) for i in range(2)]
            sB = u_t[0]
            msB = u_t[1]
            pstatB = lE(nc.psum_tensor("a_pstB", [1, L], F32))
            pbc0 = lE(nc.psum_tensor("a_pbc", [128, L], F32))
            emit_ln(k, cst, "a", x_t, 0, L, ln,
                    (xin, xsq,
                     (acc[0:1, 0:L], hsp[0:1, 0:L], ts_sb[0:1, 0:L]),
                     (u_t[0][0:1, 0:L], u_t[1][0:1, 0:L]),
                     sB, msB, pstatB, pbc0),
                    (0, 0, 0, 0, 0))
        LND = 2 * NHT

        psA = [E(nc.psum_tensor(f"a_psA{i}", [128, 512], F32)) for i in range(2)]
        psG = [E(nc.psum_tensor(f"a_psG{i}", [128, 512], F32)) for i in range(2)]
        pTS = E(nc.psum_tensor("a_pTS", [128, 512], F32))
        pBC = E(nc.psum_tensor("a_pBC", [32, 512], F32))

        def s_io(sync):
            sPRE.dma(sync, cw[:], conv_w_r.ap())
            sPRE.dma(sync, cb[:], conv_b_r.ap())
            for t in range(NDT):
                sSG.wait(sync, 4 * (t + 1))
                sOUT.dma(sync, xap(sg_s, t * 128, 0, L),
                                        sg_t[t % 2][:])
                sSIL.wait(sync, t + 1)
                sOUT.dma(sync, xap(u_s, t * 128, 0, L),
                                        u_t[t % 2][:])
            sTS.wait(sync, 4 * NDT + 2)
            sOUT.dma(sync, bc_s.ap(), bcbf[:])
            sOUT.dma(sync, ts_s.ap(), ts_bf[:])
        k.on("sync", s_io)

        def t_inproj(tensor):
            sLN.wait(tensor, LND)
            sW.wait(tensor, 32)
            for t in range(NDT):
                sW.wait(tensor, 64 + 32 * t)
                for lc in range(4):
                    g = 4 * t + lc
                    if g >= 2:
                        sHS.wait(tensor, g - 1)
                    la = None
                    for kk in range(NHT):
                        la = tensor.matmul(
                            psA[lc % 2][:],
                            wsA[t % 2][:, kk * 128:(kk + 1) * 128],
                            ln[:, kk * L + lc * 512:kk * L + lc * 512 + 512],
                            start=(kk == 0), stop=(kk == NHT - 1))
                    sMM.inc(la, 1)
                    if g >= 2:
                        sSG.wait(tensor, g - 1)
                    for kk in range(NHT):
                        la = tensor.matmul(
                            psG[lc % 2][:],
                            wsG[t % 2][:, kk * 128:(kk + 1) * 128],
                            ln[:, kk * L + lc * 512:kk * L + lc * 512 + 512],
                            start=(kk == 0), stop=(kk == NHT - 1))
                    sMM.inc(la, 1)
                sSIL.wait(tensor, t + 1)
                for lc in range(4):
                    sTS.wait(tensor, 4 * t + lc)
                    la = tensor.matmul(pTS[:], wdt[:, t * 128:(t + 1) * 128],
                                       u_t[t % 2][:, lc * 512:lc * 512 + 512],
                                       start=True, stop=True)
                    la = tensor.matmul(pBC[:], wbc[:, t * 32:(t + 1) * 32],
                                       u_t[t % 2][:, lc * 512:lc * 512 + 512],
                                       start=True, stop=True)
                    sMM.inc(la, 1)
        k.on("tensor", t_inproj)

        def s_evac(scalar):
            for t in range(NDT):
                for lc in range(4):
                    sMM.wait(scalar, 12 * t + 2 * lc + 1)
                    if lc == 0:
                        sCV.wait(scalar, t)
                    sHS.inc(scalar.copy(hsp[:, 3 + lc * 512:3 + lc * 512 + 512],
                                        psA[lc % 2][:]), 1)
                    sMM.wait(scalar, 12 * t + 2 * lc + 2)
                    if lc == 0 and t >= 2:
                        sOUT.wait(scalar, 32 * (t - 1))
                    sSG.inc(scalar.activation(
                        sg_t[t % 2][:, lc * 512:lc * 512 + 512],
                        psG[lc % 2][:], AF.Silu), 1)
                sCV.wait(scalar, t + 1)
                if t >= 2:
                    sMM.wait(scalar, 12 * (t - 1))
                sSIL.inc(scalar.activation(u_t[t % 2][:], acc[:], AF.Silu), 1)
        k.on("scalar", s_evac)

        def v_conv(vector):
            sPRE.wait(vector, 32)
            for t in range(NDT):
                sHS.wait(vector, 4 * (t + 1))
                if t >= 1:
                    sSIL.wait(vector, t)
                vector.memset(hsp[:, 0:KCV - 1], 0.0)
                vector.tensor_scalar(acc[:], hsp[:, 0:L],
                                     cw[:, KCV * t:KCV * t + 1], cb[:, t:t + 1],
                                     op0=ALU.mult, op1=ALU.add)
                last = None
                for j in range(1, KCV):
                    last = vector.scalar_tensor_tensor(
                        acc[:], hsp[:, j:j + L],
                        cw[:, KCV * t + j:KCV * t + j + 1],
                        acc[:], op0=ALU.mult, op1=ALU.add)
                sCV.inc(last, 1)
                if t == 0:
                    vector.memset(ts_sb[:], 0.0)
                    vector.memset(bc_sb[:], 0.0)
                for lc in range(4):
                    sMM.wait(vector, 12 * t + 8 + lc + 1)
                    vector.tensor_add(ts_sb[:, lc * 512:lc * 512 + 512], pTS[:],
                                      ts_sb[:, lc * 512:lc * 512 + 512])
                    sTS.inc(vector.tensor_add(
                        bc_sb[:, lc * 512:lc * 512 + 512], pBC[:],
                        bc_sb[:, lc * 512:lc * 512 + 512]), 1)
                for ui in dqu(t):
                    dq["dve"](vector, ui)
            vector.tensor_copy(ts_bf[:], ts_sb[:])
            sTS.inc(vector.tensor_copy(bcbf[:], bc_sb[:]), 2)
        k.on("vector", v_conv)


        k.entry = [sOUT, sMM, sCV] + ([dq["sDQS"]] if dq else [])


def phase_ssm(k, cst, u_s, sg_s, bc_s, ts_s, w_dtp_T, b_dtp_r, a_log_r, d_r,
              ypart_s, y2_s):
    nc = k.nc
    sPRE = k.sem("s_pre")
    sBC = k.sem("s_bcin")
    sU = k.sem("s_u")
    sDTM = k.sem("s_dtm")
    sZ = k.sem("s_z")
    sSP = k.sem("s_sp")
    sDT = k.sem("s_dt")
    sDU = k.sem("s_du")
    sDA = k.sem("s_da")
    sSCN = k.sem("s_scn")
    sPE = k.sem("s_pe")
    sYE = k.sem("s_ye")
    sST = k.sem("s_st")

    with contextlib.ExitStack() as ctx:
        E = ctx.enter_context
        wdtp = E(nc.sbuf_tensor("s_wdtp", [128, DI], BF16))
        bdtp = E(nc.sbuf_tensor("s_bdtp", [128, NDT], F32))
        Asb = E(nc.sbuf_tensor("s_A", [128, NDT * NST], F32))
        Dsb = E(nc.sbuf_tensor("s_D", [128, NDT], F32))
        tsb = E(nc.sbuf_tensor("s_ts", [128, L], BF16))
        bcast = E(nc.sbuf_tensor("s_bcast", [128, 8 * L], BF16))
        ccast = E(nc.sbuf_tensor("s_ccast", [128, 8 * L], BF16))
        u_t = [E(nc.sbuf_tensor(f"s_u{i}", [128, L], BF16)) for i in range(2)]
        sg_t = [E(nc.sbuf_tensor(f"s_sg{i}", [128, L], BF16)) for i in range(2)]
        yp_t = [E(nc.sbuf_tensor(f"s_yp{i}", [128, L], F32)) for i in range(2)]
        z = E(nc.sbuf_tensor("s_z", [128, L], F32))
        tmp = E(nc.sbuf_tensor("s_tmp", [128, L], F32))
        rl = E(nc.sbuf_tensor("s_rl", [128, L], F32))
        dt = E(nc.sbuf_tensor("s_dt", [128, L], F32))
        du = E(nc.sbuf_tensor("s_du", [128, L], BF16))
        dA = [E(nc.sbuf_tensor(f"s_dA{i}", [128, L], F32)) for i in range(2)]
        dBu = E(nc.sbuf_tensor("s_dBu", [128, L], BF16))
        hh = E(nc.sbuf_tensor("s_h", [128, L], BF16))
        ch = E(nc.sbuf_tensor("s_ch", [128, L], BF16))
        t0 = E(nc.sbuf_tensor("s_t0", [128, L], F32))
        yo = E(nc.sbuf_tensor("s_yo", [128, L], F32))
        y2o = E(nc.sbuf_tensor("s_y2o", [128, L], BF16))
        pDT = [E(nc.psum_tensor(f"s_pDT{i}", [128, 512], F32)) for i in range(2)]
        pY = E(nc.psum_tensor("s_pY", [128, L], F32))
        k.phase_entry()

        def s_pre(sync):
            sPRE.dma(sync, 
                z[:], bass.AP(w_dtp_T, 0, [[DI, 128], [1, 2048]]))
            sPRE.dma(sync, 
                tmp[:], bass.AP(w_dtp_T, 2048, [[DI, 128], [1, 2048]]))
            sPRE.dma(sync, bdtp[:], b_dtp_r.ap())
            sPRE.dma(sync, Asb[:], a_log_r.ap())
            sPRE.dma(sync, Dsb[:], d_r.ap())
            sPRE.dma(sync, tsb[:], ts_s.ap())
        k.on("sync", s_pre)

        def s_A(scalar):
            sPRE.wait(scalar, 96)
            sPRE.inc(scalar.activation(Asb[:], Asb[:], AF.Exp), 1)
        k.on("scalar", s_A)

        def v_A(vector):
            sPRE.wait(vector, 97)
            vector.tensor_scalar_mul(Asb[:], Asb[:], -1.0)
            vector.tensor_copy(wdtp[:, 0:2048], z[:])
            sPRE.inc(vector.tensor_copy(wdtp[:, 2048:4096], tmp[:]), 1)
        k.on("vector", v_A)
        APRE = 98

        for half in range(2):
            UPT = 1 if half == 0 else 3
            U0 = 16 * NDT * half

            def s_io(sync, half=half):
                if half == 1:
                    sSCN.wait(sync, 2 * 8 * NDT)
                for nn in range(8):
                    na = half * 8 + nn
                    sBC.dma(sync, bcast[:, nn * L:(nn + 1) * L],
                                           bass.AP(bc_s, na * L,
                                                   [[0, 128], [1, L]]))
                    sBC.dma(sync, ccast[:, nn * L:(nn + 1) * L],
                                           bass.AP(bc_s, (NST + na) * L,
                                                   [[0, 128], [1, L]]))

                def load(t):
                    if t >= NDT:
                        return
                    it = half * NDT + t
                    if it >= 2:
                        sYE.wait(sync, 3 * (it - 1))
                    sU.dma(sync, u_t[t % 2][:],
                                          xap(u_s, t * 128, 0, L))
                    if half == 1:
                        sU.dma(sync, sg_t[t % 2][:],
                                              xap(sg_s, t * 128, 0, L))
                        sU.dma(sync, yp_t[t % 2][:],
                                              xap(ypart_s, t * 128, 0, L))
                load(0)
                load(1)
                for t in range(NDT):
                    it = half * NDT + t
                    load(t + 2)
                    sYE.wait(sync, 3 * (it + 1))
                    if half == 0:
                        sST.dma(sync, xap(ypart_s, t * 128, 0, L),
                                               yo[:])
                    else:
                        sST.dma(sync, xap(y2_s, t * 128, 0, L),
                                               y2o[:])
            k.on("sync", s_io)
            BC0 = 16 * 16 * (half + 1)

            def t_main(tensor, half=half):
                if half == 0:
                    sPRE.wait(tensor, APRE)
                for t in range(NDT):
                    it = half * NDT + t
                    for lc in range(4):
                        g = 4 * it + lc
                        if g >= 2:
                            sZ.wait(tensor, g - 1)
                        sDTM.inc(tensor.matmul(
                            pDT[lc % 2][:], wdtp[:, t * 128:(t + 1) * 128],
                            tsb[:, lc * 512:lc * 512 + 512],
                            start=True, stop=True), 1)
                    for nn in range(8):
                        gg = 8 * it + nn
                        sSCN.wait(tensor, 2 * gg + 2)
                        la = None
                        for lc in range(4):
                            la = tensor.matmul(pY[:, lc * 512:lc * 512 + 512],
                                               cst["ident"][:],
                                               ch[:, lc * 512:lc * 512 + 512],
                                               start=(nn == 0), stop=(nn == 7))
                        sPE.inc(la, 1)
                    if it < 2 * NDT - 1:
                        sYE.wait(tensor, 3 * it + 1)
            k.on("tensor", t_main)

            def s_act(scalar, half=half):
                for t in range(NDT):
                    it = half * NDT + t
                    for lc in range(4):
                        sDTM.wait(scalar, 4 * it + lc + 1)
                        if lc == 0 and it >= 1:
                            sDA.wait(scalar, 8 * it)
                            sDU.wait(scalar, it)
                            sDT.wait(scalar, it)
                        sZ.inc(scalar.activation(z[:, lc * 512:lc * 512 + 512],
                                                 pDT[lc % 2][:], AF.Identity,
                                                 bias=bdtp[:, t:t + 1]), 1)
                    scalar.activation(tmp[:], z[:], AF.Abs)
                    scalar.activation(tmp[:], tmp[:], AF.Exp, scale=-1.0)
                    sSP.inc(scalar.activation(tmp[:], tmp[:], AF.Ln, bias=1.0), 3)
                    for nn in range(8):
                        na = half * 8 + nn
                        gg = 8 * it + nn
                        sDT.wait(scalar, it + 1)
                        if gg >= 2:
                            sSCN.wait(scalar, 2 * gg - 3)
                        sDA.inc(scalar.activation(
                            dA[nn % 2][:], dt[:], AF.Exp,
                            scale=Asb[:, t * NST + na:t * NST + na + 1]), 1)
            k.on("scalar", s_act)

            def v_main(vector, half=half, U0=U0, UPT=UPT, BC0=BC0):
                for t in range(NDT):
                    it = half * NDT + t
                    sZ.wait(vector, 4 * it + 4)
                    vector.tensor_scalar_max(rl[:], z[:], 0.0)
                    sSP.wait(vector, 3 * (it + 1))
                    sDT.inc(vector.tensor_add(dt[:], tmp[:], rl[:]), 1)
                    sU.wait(vector, U0 + 16 * UPT * t + 16)
                    sDU.inc(vector.tensor_mul(du[:], dt[:], u_t[t % 2][:]), 1)
                    for nn in range(8):
                        gg = 8 * it + nn
                        sDA.wait(vector, gg + 1)
                        if t == 0 and nn == 0:
                            sBC.wait(vector, BC0)
                        vector.tensor_mul(dBu[:], du[:],
                                          bcast[:, nn * L:(nn + 1) * L])
                        sSCN.inc(vector.tensor_tensor_scan(
                            hh[:], dA[nn % 2][:], dBu[:], 0.0,
                            op0=ALU.mult, op1=ALU.add), 1)
                        if gg >= 1:
                            sPE.wait(vector, gg)
                        sSCN.inc(vector.tensor_mul(
                            ch[:], hh[:], ccast[:, nn * L:(nn + 1) * L]), 1)
                    sPE.wait(vector, 8 * (it + 1))
                    if half == 0:
                        if it >= 1:
                            sST.wait(vector, 16 * it)
                        sYE.inc(vector.tensor_copy(yo[:], pY[:]), 3)
                    else:
                        sST.wait(vector, 16 * NDT + 16 * t)
                        sYE.inc(vector.scalar_tensor_tensor(
                            t0[:], u_t[t % 2][:], Dsb[:, t:t + 1], pY[:],
                            op0=ALU.mult, op1=ALU.add), 1)
                        sU.wait(vector, U0 + 48 * t + 48)
                        vector.tensor_add(t0[:], t0[:], yp_t[t % 2][:])
                        sYE.inc(vector.tensor_mul(y2o[:], t0[:],
                                                  sg_t[t % 2][:]), 2)
            k.on("vector", v_main)

        k.entry = [sST, sPE, sSCN]


def phase_proj_res(k, cst, tag, w_T, KD, rhs_s, x_src, dst, bias_r=None):
    nc = k.nc
    NKT = KD // 128
    sW = k.sem(tag + "_w")
    sWC = k.sem(tag + "_wc")
    sMM = k.sem(tag + "_mm")
    sEV = k.sem(tag + "_ev")
    sX = k.sem(tag + "_x")
    sST = k.sem(tag + "_st")

    with contextlib.ExitStack() as ctx:
        E = ctx.enter_context
        yfull = E(nc.sbuf_tensor(tag + "_yf", [128, NKT * L], BF16))
        ws = [E(nc.sbuf_tensor(f"{tag}_ws{i}", [128, NKT * 128], BF16))
              for i in range(2)]
        wsf = [E(nc.sbuf_tensor(f"{tag}_wsf{i}", [128, NKT * 128], F32))
               for i in range(2)]
        xo = [E(nc.sbuf_tensor(f"{tag}_xo{i}", [128, L], F32)) for i in range(2)]
        bsb = E(nc.sbuf_tensor(tag + "_b", [128, NHT], F32))
        ps = [E(nc.psum_tensor(f"{tag}_ps{i}", [128, 512], F32)) for i in range(2)]
        k.phase_entry()

        def ws_dma(sync, o):
            if o >= NHT:
                return
            if o >= 2:
                sWC.wait(sync, o - 1)
            sW.dma(sync, 
                wsf[o % 2][:].rearrange("p (t o) -> p t o", t=NKT),
                bass.AP(w_T, o * 128, [[H, 128], [128 * H, NKT], [1, 128]]))

        def s_io(sync):
            if bias_r is not None:
                sW.dma(sync, bsb[:], bias_r.ap())
            for t in range(NKT):
                sX.dma(sync, yfull[:, t * L:(t + 1) * L],
                                      xap(rhs_s, t * 128, 0, L))
            ws_dma(sync, 0)
            ws_dma(sync, 1)
            sX.dma(sync, xo[0][:], xap(x_src, 0, 0, L))
            sX.dma(sync, xo[1][:], xap(x_src, 128, 0, L))
            for o in range(NHT):
                ws_dma(sync, o + 2)
                sEV.wait(sync, 4 * (o + 1))
                sST.dma(sync, xap(dst, o * 128, 0, L), xo[o % 2][:])
                if o + 2 < NHT:
                    sST.wait(sync, 16 * (o + 1))
                    sX.dma(sync, xo[o % 2][:],
                                          xap(x_src, (o + 2) * 128, 0, L))
        k.on("sync", s_io)
        WB = 16 if bias_r is not None else 0

        def t_mm(tensor):
            sX.wait(tensor, 16 * NKT)
            for o in range(NHT):
                sWC.wait(tensor, o + 1)
                for lc in range(4):
                    g = 4 * o + lc
                    if g >= 2:
                        sEV.wait(tensor, g - 1)
                    la = None
                    for kk in range(NKT):
                        la = tensor.matmul(
                            ps[lc % 2][:], ws[o % 2][:, kk * 128:(kk + 1) * 128],
                            yfull[:, kk * L + lc * 512:kk * L + lc * 512 + 512],
                            start=(kk == 0), stop=(kk == NKT - 1))
                    sMM.inc(la, 1)
        k.on("tensor", t_mm)

        def ws_conv(vector, o):
            if o >= NHT:
                return
            sW.wait(vector, WB + 16 * (o + 1))
            if o >= 2:
                sMM.wait(vector, 4 * (o - 1))
            sWC.inc(vector.tensor_copy(ws[o % 2][:], wsf[o % 2][:]), 1)

        def v_ev(vector):
            ws_conv(vector, 0)
            ws_conv(vector, 1)
            for o in range(NHT):
                sX.wait(vector, 16 * NKT + 16 * (o + 1))
                if o >= 2:
                    sST.wait(vector, 16 * (o - 1))
                for lc in range(4):
                    sMM.wait(vector, 4 * o + lc + 1)
                    sl = slice(lc * 512, lc * 512 + 512)
                    if bias_r is not None:
                        sEV.inc(vector.scalar_tensor_tensor(
                            xo[o % 2][:, sl], ps[lc % 2][:], bsb[:, o:o + 1],
                            xo[o % 2][:, sl], op0=ALU.add, op1=ALU.add), 1)
                    else:
                        sEV.inc(vector.tensor_add(xo[o % 2][:, sl],
                                                  ps[lc % 2][:],
                                                  xo[o % 2][:, sl]), 1)
                ws_conv(vector, o + 2)
        k.on("vector", v_ev)

        def ws_conv(vector, o):
            if o >= NHT:
                return
            sW.wait(vector, WB + 16 * (o + 1))
            if o >= 2:
                sMM.wait(vector, 4 * (o - 1))
            sWC.inc(vector.tensor_copy(ws[o % 2][:], wsf[o % 2][:]), 1)

        def v_ev(vector):
            ws_conv(vector, 0)
            ws_conv(vector, 1)
            for o in range(NHT):
                ws_conv(vector, o + 2)
                sX.wait(vector, 16 * NKT + 16 * (o + 1))
                if o >= 2:
                    sST.wait(vector, 16 * (o - 1))
        k.entry = [sST, sMM]


def phase_ffn(k, cst, tag, x_src, wgu_s, wdn_s, dst):
    nc = k.nc
    LC = 1024
    sX = k.sem(tag + "_x")
    sSTT = k.sem(tag + "_stat")
    sROW = k.sem(tag + "_row")
    sBC = k.sem(tag + "_bc")
    sLN = k.sem(tag + "_ln")
    sWG = k.sem(tag + "_wg")
    sWD = k.sem(tag + "_wd")
    sMG = k.sem(tag + "_mg")
    sSI = k.sem(tag + "_si")
    sHM = k.sem(tag + "_hm")
    sMD = k.sem(tag + "_md")
    sYE = k.sem(tag + "_ye")
    sRS = k.sem(tag + "_rs")
    sRB = k.sem(tag + "_rb")
    sFN = k.sem(tag + "_fn")
    sST = k.sem(tag + "_st")
    sXO = sX

    ones1, onesc = cst["ones1"], cst["onesc"]

    with contextlib.ExitStack() as ctx:
        E = ctx.enter_context
        lnyb = E(nc.sbuf_tensor(f"{tag}_lnyb", [128, NHT * LC], BF16))
        wg = [E(nc.sbuf_tensor(f"{tag}_wg{i}", [128, NHT * 128], BF16))
              for i in range(3)]
        wd = [E(nc.sbuf_tensor(f"{tag}_wd{i}", [128, NIT * 128], BF16))
              for i in range(2)]
        sg = E(nc.sbuf_tensor(f"{tag}_sg", [128, LC], BF16))
        hi = E(nc.sbuf_tensor(f"{tag}_hi", [128, NIT * LC], BF16))
        ysq = [E(nc.sbuf_tensor(f"{tag}_ysq{i}", [128, LC], BF16))
               for i in range(2)]
        rB = E(nc.sbuf_tensor(f"{tag}_rB", [128, LC], BF16))
        xo = [E(nc.sbuf_tensor(f"{tag}_xo{i}", [128, LC], F32)) for i in range(2)]
        t1 = E(nc.sbuf_tensor(f"{tag}_t1", [128, LC], F32))
        k.phase_entry()

        for lci in range(2):
            l0 = lci * LC
            if lci == 1:
                for e in ("sync", "gpsimd", "tensor", "scalar", "vector"):
                    def f(eng):
                        sST.wait(eng, 16 * NHT)
                    k.on(e, f)

            with contextlib.ExitStack() as lctx:
                lE = lctx.enter_context
                xin = [lE(nc.sbuf_tensor(f"{tag}_xin{lci}_{i}", [128, LC], F32))
                       for i in range(2)]
                xsq = [lE(nc.sbuf_tensor(f"{tag}_xsq{lci}_{i}", [128, LC], BF16))
                       for i in range(2)]
                sB = lE(nc.sbuf_tensor(f"{tag}_sB{lci}", [128, LC], BF16))
                msB = lE(nc.sbuf_tensor(f"{tag}_msB{lci}", [128, LC], BF16))
                pstatB = lE(nc.psum_tensor(f"{tag}_psB{lci}", [1, LC], F32))
                pbc = lE(nc.psum_tensor(f"{tag}_pbc{lci}", [128, LC], F32))
                emit_ln(k, cst, tag, x_src, l0, LC, lnyb,
                        (xin, xsq,
                         (t1[0:1, 0:LC], xo[0][0:1, 0:LC], xo[1][0:1, 0:LC]),
                         (ysq[0][0:1, 0:LC], ysq[1][0:1, 0:LC]),
                         sB, msB, pstatB, pbc),
                        (32 * NHT * lci, (NHT + 2) * lci, (NHT + 5) * lci,
                         3 * lci, 2 * NHT * lci))
            LND = 2 * NHT * lci + 2 * NHT
            RR = (NHT + 5) * lci + NHT + 1

            WG0 = 16 * 2 * NIT * lci
            MG0 = 4 * NIT * lci
            SI0 = NIT * lci
            HM0 = NIT * lci

            with contextlib.ExitStack() as pctx:
                pg = [pctx.enter_context(
                    nc.psum_tensor(f"{tag}_pg{lci}_{i}", [128, LC], F32))
                    for i in range(2)]

                def g_wg(g, lci=lci, MG0=MG0):
                    for i2 in range(2 * NIT):
                        if i2 >= 3:
                            sMG.wait(g, MG0 + 2 * (i2 - 2))
                        pi, gu = i2 // 2, i2 % 2
                        oi = pi + gu * NIT
                        sWG.dma(g, 
                            wg[i2 % 3][:].rearrange("p (t o) -> p t o", t=NHT),
                            bass.AP(wgu_s, oi * 128,
                                    [[2 * IFF, 128], [128 * 2 * IFF, NHT],
                                     [1, 128]]))
                k.on("gpsimd", g_wg)

                def t_gu(tensor, LND=LND, WG0=WG0, SI0=SI0, HM0=HM0):
                    sLN.wait(tensor, LND)
                    for pi in range(NIT):
                        for gu in range(2):
                            i2 = 2 * pi + gu
                            sWG.wait(tensor, WG0 + 16 * (i2 + 1))
                            if gu == 0 and pi >= 1:
                                sSI.wait(tensor, SI0 + pi)
                            if gu == 1 and pi >= 1:
                                sHM.wait(tensor, HM0 + pi)
                            la = None
                            for j in range(2):
                                for kk in range(NHT):
                                    la = tensor.matmul(
                                        pg[gu][:, j * 512:(j + 1) * 512],
                                        wg[i2 % 3][:, kk * 128:(kk + 1) * 128],
                                        lnyb[:, kk * LC + j * 512:kk * LC + j * 512 + 512],
                                        start=(kk == 0), stop=(kk == NHT - 1))
                            sMG.inc(la, 2)
                    sSI.wait(tensor, SI0 + NIT)
                    sHM.wait(tensor, HM0 + NIT)
                k.on("tensor", t_gu)

                def s_silu(scalar, MG0=MG0, HM0=HM0):
                    for pi in range(NIT):
                        sMG.wait(scalar, MG0 + 4 * pi + 2)
                        if pi >= 1:
                            sHM.wait(scalar, HM0 + pi)
                        sSI.inc(scalar.activation(sg[:], pg[0][:], AF.Silu), 1)
                k.on("scalar", s_silu)

                def v_h(vector, lci=lci, MG0=MG0, SI0=SI0):
                    if lci == 1:
                        sMD.wait(vector, NHT)
                    for pi in range(NIT):
                        sMG.wait(vector, MG0 + 4 * pi + 4)
                        sSI.wait(vector, SI0 + pi + 1)
                        sHM.inc(vector.tensor_mul(hi[:, pi * LC:(pi + 1) * LC],
                                                  pg[1][:], sg[:]), 1)
                k.on("vector", v_h)

            WD0 = 16 * NHT * lci
            MD0 = NHT * lci
            YE0 = 2 * NHT * lci
            RS0 = 17 * lci

            with contextlib.ExitStack() as pctx:
                pd = [pctx.enter_context(
                    nc.psum_tensor(f"{tag}_pd{lci}_{i}", [128, LC], F32))
                    for i in range(2)]
                prs = pctx.enter_context(
                    nc.psum_tensor(f"{tag}_prs{lci}", [1, LC], F32))
                pbc2 = pctx.enter_context(
                    nc.psum_tensor(f"{tag}_pbc2{lci}", [128, LC], F32))

                def g_wd(g, lci=lci, MD0=MD0):
                    for ho in range(NHT):
                        if ho >= 2:
                            sMD.wait(g, MD0 + ho - 1)
                        sWD.dma(g, 
                            wd[ho % 2][:].rearrange("p (t o) -> p t o", t=NIT),
                            bass.AP(wdn_s, ho * 128,
                                    [[H, 128], [128 * H, NIT], [1, 128]]))
                k.on("gpsimd", g_wd)

                def t_dn(tensor, lci=lci, WD0=WD0, MD0=MD0, YE0=YE0, RR=RR):
                    sHM.wait(tensor, NIT * lci + NIT)
                    for ho in range(NHT):
                        sWD.wait(tensor, WD0 + 16 * (ho + 1))
                        if ho >= 2:
                            sYE.wait(tensor, YE0 + 2 * (ho - 1))
                        la = None
                        for j in range(2):
                            for kk in range(NIT):
                                la = tensor.matmul(
                                    pd[ho % 2][:, j * 512:(j + 1) * 512],
                                    wd[ho % 2][:, kk * 128:(kk + 1) * 128],
                                    hi[:, kk * LC + j * 512:kk * LC + j * 512 + 512],
                                    start=(kk == 0), stop=(kk == NIT - 1))
                        sMD.inc(la, 1)
                        if ho >= 1:
                            sYE.wait(tensor, YE0 + 2 * ho)
                            la = None
                            for j in range(2):
                                la = tensor.matmul(
                                    prs[0:1, j * 512:(j + 1) * 512], onesc[:],
                                    ysq[(ho - 1) % 2][:, j * 512:(j + 1) * 512],
                                    start=(ho == 1), stop=False)
                            sRS.inc(la, 1)
                    sYE.wait(tensor, YE0 + 2 * NHT)
                    la = None
                    for j in range(2):
                        la = tensor.matmul(prs[0:1, j * 512:(j + 1) * 512],
                                           onesc[:],
                                           ysq[(NHT - 1) % 2][:, j * 512:(j + 1) * 512],
                                           start=False, stop=True)
                    sRS.inc(la, 1)
                    sROW.wait(tensor, RR + 4)
                    sBC.wait(tensor, 3 * lci + 3)
                    la = None
                    for j in range(2):
                        la = tensor.matmul(pbc2[:, j * 512:(j + 1) * 512],
                                           ones1[:],
                                           ysq[0][0:1, j * 512:(j + 1) * 512],
                                           start=True, stop=True)
                    sRS.inc(la, 1)
                k.on("tensor", t_dn)

                def s_yev(scalar, lci=lci, MD0=MD0, RS0=RS0, YE0=YE0):
                    for ho in range(NHT):
                        sMD.wait(scalar, MD0 + ho + 1)
                        if ho >= 2:
                            sRS.wait(scalar, RS0 + ho - 1)
                        if lci == 1 and ho < 2:
                            sFN.wait(scalar, NHT)
                        scalar.activation(ysq[ho % 2][:], pd[ho % 2][:], AF.Square)
                        sYE.inc(scalar.copy(lnyb[:, ho * LC:(ho + 1) * LC],
                                            pd[ho % 2][:]), 2)
                    sRS.wait(scalar, RS0 + NHT)
                    rr = t1[0:1, 0:LC]
                    scalar.copy(rr, prs[0:1, 0:LC])
                    scalar.activation(rr, rr, AF.Ln, scale=1.0 / H, bias=RMSEPS)
                    sROW.inc(scalar.activation(rr, rr, AF.Exp, scale=-0.5), 1)
                    sRS.wait(scalar, RS0 + NHT + 1)
                    sRB.inc(scalar.copy(rB[:], pbc2[:, 0:LC]), 1)
                k.on("scalar", s_yev)

                def v_rinv(vector, RR=RR):
                    sROW.wait(vector, RR + 4)
                    sBC.inc(vector.tensor_copy(ysq[0][0:1, 0:LC],
                                               t1[0:1, 0:LC]), 1)
                k.on("vector", v_rinv)

                def s_xio(sync, lci=lci, l0=l0, RR=RR):
                    sROW.wait(sync, RR + 3)
                    sXO.dma(sync, xo[0][:],
                                           xap(x_src, 0, l0, LC))
                    sXO.dma(sync, xo[1][:],
                                           xap(x_src, 128, l0, LC))
                    for ho in range(NHT):
                        sFN.wait(sync, lci * NHT + ho + 1)
                        sST.dma(sync, xap(dst, ho * 128, l0, LC),
                                               xo[ho % 2][:])
                        if ho + 2 < NHT:
                            sST.wait(sync, 16 * (lci * NHT + ho + 1))
                            sXO.dma(sync, 
                                xo[ho % 2][:],
                                xap(x_src, (ho + 2) * 128, l0, LC))
                k.on("sync", s_xio)
                XO0 = 32 * NHT * lci + 16 * NHT

                def v_fin(vector, lci=lci, XO0=XO0, YE0=YE0):
                    sRB.wait(vector, lci + 1)
                    sYE.wait(vector, YE0 + 2 * NHT)
                    for ho in range(NHT):
                        sXO.wait(vector, XO0 + 16 * (ho + 1))
                        if ho >= 2:
                            sST.wait(vector, 16 * (lci * NHT + ho - 1))
                        vector.tensor_mul(t1[:], lnyb[:, ho * LC:(ho + 1) * LC],
                                          rB[:])
                        sFN.inc(vector.tensor_add(xo[ho % 2][:], t1[:],
                                                  xo[ho % 2][:]), 1)
                k.on("vector", v_fin)

        k.entry = [sST, sFN]


ATTN_DEBUG = False


def phase_attn(k, cst, x_src, wq_T, wk_T, wv_T, bq_r, bk_r, bv_r,
               e_sel, mdiag, mdiag1, aout_s):
    nc = k.nc
    adbg = nc.dram_tensor("adbg", [4 * 128, L], BF16,
                          kind="ExternalOutput" if ATTN_DEBUG else "Internal")
    adbf = nc.dram_tensor("adbf", [4 * 128, L], F32,
                          kind="ExternalOutput" if ATTN_DEBUG else "Internal")
    sDBG = k.sem("at_dbg") if ATTN_DEBUG else None
    sLN = k.sem("at_ln")
    sPB = k.sem("at_pb")
    sW = k.sem("at_w")
    sWC = k.sem("at_wc")
    sQK = k.sem("at_qk")
    sQE = k.sem("at_qe")
    sS1 = k.sem("at_s1")
    sM1 = k.sem("at_m1")
    sE1 = k.sem("at_e1")
    sSD = k.sem("at_sd")
    sS2 = k.sem("at_s2")
    sA2 = k.sem("at_a2")
    sF2 = k.sem("at_f2")
    sP2 = k.sem("at_p2")
    sPV = k.sem("at_pv")
    sAO = k.sem("at_ao")
    sST = k.sem("at_st")

    ones1, onesc = cst["ones1"], cst["onesc"]
    SC = float(DH) ** -0.25
    NKTQ = [4 * qc + 4 for qc in range(4)]
    BASEQ = [sum(NKTQ[:qc]) for qc in range(4)]
    BPH = sum(NKTQ)

    with contextlib.ExitStack() as ctx:
        E = ctx.enter_context
        ln = E(nc.sbuf_tensor("at_lnf", [128, NHT * L], BF16))
        esl = E(nc.sbuf_tensor("at_esl", [NST, NST * 128], BF16))
        mdg = E(nc.sbuf_tensor("at_mdg", [128, 4 * 512], BF16))
        mdg1 = E(nc.sbuf_tensor("at_mdg1", [128, 128], F32))
        wqs = [E(nc.sbuf_tensor(f"at_wqs{i}", [128, NHT * 128], BF16))
               for i in range(2)]
        wks = [E(nc.sbuf_tensor(f"at_wks{i}", [128, NHT * 128], BF16))
               for i in range(2)]
        wvs = [E(nc.sbuf_tensor(f"at_wvs{i}", [128, NHT * 128], BF16))
               for i in range(2)]
        wsf = [E(nc.sbuf_tensor(f"at_wsf{i}", [128, NHT * 128], F32))
               for i in range(2)]
        bqs = E(nc.sbuf_tensor("at_bqs", [128, NHT], F32))
        bks = E(nc.sbuf_tensor("at_bks", [128, NHT], F32))
        bvs = E(nc.sbuf_tensor("at_bvs", [128, NHT], F32))
        pexp = E(nc.sbuf_tensor("at_pexp", [128, L], BF16))
        seedBd = E(nc.sbuf_tensor("at_seedBd", [128, 4 * 512], F32))
        pT = E(nc.sbuf_tensor("at_pT", [128, NST * 512], BF16))
        k.phase_entry()

        with contextlib.ExitStack() as lctx:
            lE = lctx.enter_context
            xin = [lE(nc.sbuf_tensor(f"at_xin{i}", [128, L], F32)) for i in range(2)]
            xsq = [lE(nc.sbuf_tensor(f"at_xsq{i}", [128, L], BF16)) for i in range(2)]
            sB = lE(nc.sbuf_tensor("at_sB", [128, L], BF16))
            msB = lE(nc.sbuf_tensor("at_msB", [128, L], BF16))
            pstatB = lE(nc.psum_tensor("at_psB", [1, L], F32))
            pbc = lE(nc.psum_tensor("at_pbc", [128, L], F32))
            emit_ln(k, cst, "at", x_src, 0, L, ln,
                    (xin, xsq,
                     (wsf[0][0:1, 0:L], wsf[1][0:1, 0:L], seedBd[0:1, 0:L]),
                     (pexp[0:1, 0:L], pT[0:1, 0:L]),
                     sB, msB, pstatB, pbc),
                    (0, 0, 0, 0, 0))
        LND = 2 * NHT

        qT = [E(nc.sbuf_tensor(f"at_qT{i}", [128, L], BF16)) for i in range(2)]
        kT = [E(nc.sbuf_tensor(f"at_kT{i}", [128, L], BF16)) for i in range(2)]
        vtr = [E(nc.sbuf_tensor(f"at_vtr{i}", [128, L], BF16)) for i in range(2)]
        ncol = E(nc.sbuf_tensor("at_ncol", [128, 3 * NST], F32))
        sdc = E(nc.sbuf_tensor("at_sdc", [128, NST], F32))
        seeds = E(nc.sbuf_tensor("at_seeds", [NST, 128], BF16))
        seedB = E(nc.sbuf_tensor("at_seedB", [128, 512], F32))
        s2f = [E(nc.sbuf_tensor(f"at_s2f{i}", [128, 512], F32)) for i in range(2)]
        ao = [E(nc.sbuf_tensor(f"at_ao{i}", [128, L], BF16)) for i in range(2)]

        pqkv = [E(nc.psum_tensor(f"at_pqkv{i}", [128, 512], F32)) for i in range(2)]
        ps1 = E(nc.psum_tensor("at_ps1", [128, L], F32))
        ps2 = E(nc.psum_tensor("at_ps2", [128, 512], F32))
        pmix = E(nc.psum_tensor("at_pmix", [128, 512], F32))

        sROW = k.sem("at_row")

        def strip_dma(sync, d):
            if d >= 3 * NHEAD:
                return
            h, wi = d // 3, d % 3
            if d < 4:
                sROW.wait(sync, NHT + 4)
            if d >= 2:
                sWC.wait(sync, d - 1)
            wsrc = [wq_T, wk_T, wv_T][wi]
            sW.dma(sync, 
                wsf[d % 2][:].rearrange("p (t o) -> p t o", t=NHT),
                bass.AP(wsrc, h * 128, [[H, 128], [128 * H, NHT], [1, 128]]))

        def s_io(sync):
            sPB.dma(sync, esl[:], e_sel.ap())
            sPB.dma(sync, mdg[:], mdiag.ap())
            sPB.dma(sync, mdg1[:], mdiag1.ap())
            for d, s in [(bq_r, bqs), (bk_r, bks), (bv_r, bvs)]:
                sPB.dma(sync, s[:], d.ap())
            for d in range(4):
                strip_dma(sync, d)
            for h in range(NHEAD):
                for wi in range(3):
                    strip_dma(sync, 3 * h + 4 + wi)
                sAO.wait(sync, 4 * (h + 1))
                sST.dma(sync, xap(aout_s, h * 128, 0, L), ao[h % 2][:])
        k.on("sync", s_io)

        def v_bs(vector):
            sPB.wait(vector, 96)
            vector.tensor_scalar_mul(bqs[:], bqs[:], SC)
            sPB.inc(vector.tensor_scalar_mul(bks[:], bks[:], SC), 1)
        k.on("vector", v_bs)
        PB0 = 97

        def mk_t_qkv(h):
            def t_qkv(tensor, h=h):
                if h == 0:
                    sLN.wait(tensor, LND)
                    sPB.wait(tensor, PB0)
                sWC.wait(tensor, 3 * (h + 1))
                if h >= 2:
                    sAO.wait(tensor, 4 * (h - 1))
                def evac_wait(g):
                    if g >= 2:
                        sQE.wait(tensor, g - 1)
                for wi in range(2):
                    ws = [wqs, wks][wi]
                    for lc in range(4):
                        g = 24 * h + 4 * wi + lc
                        evac_wait(g)
                        la = None
                        for kk in range(NHT):
                            la = tensor.matmul(
                                pqkv[g % 2][:],
                                ws[h % 2][:, kk * 128:(kk + 1) * 128],
                                ln[:, kk * L + lc * 512:kk * L + lc * 512 + 512],
                                start=(kk == 0), stop=(kk == NHT - 1))
                        sQK.inc(la, 1)
                for lt in range(NHT):
                    g = 24 * h + 8 + lt
                    evac_wait(g)
                    la = None
                    for kk in range(NHT):
                        la = tensor.matmul(
                            pqkv[g % 2][:, 0:128],
                            ln[:, kk * L + lt * 128:kk * L + lt * 128 + 128],
                            wvs[h % 2][:, kk * 128:(kk + 1) * 128],
                            start=(kk == 0), stop=(kk == NHT - 1))
                    sQK.inc(la, 1)
            return t_qkv
        k.on("tensor", mk_t_qkv(0))

        def mk_v_wc(h):
            def v_wc(vector, h=h):
                for wi in range(3):
                    d = 3 * h + wi
                    sW.wait(vector, 16 * (d + 1))
                    dst = [wqs, wks, wvs][wi][h % 2]
                    if h >= 2:
                        sQK.wait(vector, 24 * (h - 1))
                    sWC.inc(vector.tensor_copy(dst[:], wsf[d % 2][:]), 1)
            return v_wc
        k.on("vector", mk_v_wc(0))
        k.on("vector", mk_v_wc(1))

        for h in range(NHEAD):
            if h >= 1:
                k.on("tensor", mk_t_qkv(h))

            def s_qkv_ev(scalar, h=h):
                for wi, (dstt, bb, sc) in enumerate([(qT, bqs, SC),
                                                     (kT, bks, SC)]):
                    for lc in range(4):
                        g = 24 * h + 4 * wi + lc
                        sQK.wait(scalar, g + 1)
                        sQE.inc(scalar.activation(
                            dstt[h % 2][:, lc * 512:lc * 512 + 512],
                            pqkv[g % 2][:], AF.Identity, bias=bb[:, h:h + 1],
                            scale=sc), 1)
                for lt in range(NHT):
                    g = 24 * h + 8 + lt
                    sQK.wait(scalar, g + 1)
                    sQE.inc(scalar.copy(vtr[h % 2][:, lt * 128:(lt + 1) * 128],
                                        pqkv[g % 2][:, 0:128]), 1)
            k.on("scalar", s_qkv_ev)

            def t_p1(tensor, h=h):
                for qt in range(NST):
                    kw = (qt + 1) * 128
                    if qt == 0:
                        sQE.wait(tensor, 24 * h + 8)
                    if qt >= 1:
                        sE1.wait(tensor, 17 * h + qt)
                    la = None
                    for kc in range((kw + 511) // 512):
                        cw_ = min(512, kw - kc * 512)
                        la = tensor.matmul(ps1[:, kc * 512:kc * 512 + cw_],
                                           qT[h % 2][:, qt * 128:(qt + 1) * 128],
                                           kT[h % 2][:, kc * 512:kc * 512 + cw_],
                                           start=True, stop=True)
                    sS1.inc(la, 1)
            k.on("tensor", t_p1)

            def v_p1(vector, h=h):
                for qt in range(NST):
                    kw = (qt + 1) * 128
                    sS1.wait(vector, NST * h + qt + 1)
                    if qt == 0 and h >= 1:
                        sE1.wait(vector, 17 * h)
                    vector.tensor_add(ps1[:, qt * 128:kw],
                                      ps1[:, qt * 128:kw], mdg1[:])
                    sM1.inc(vector.tensor_reduce(
                        ncol[:, qt:qt + 1], ps1[:, 0:kw],
                        axis=mybir.AxisListType.X, op=ALU.max, negate=True), 1)
            k.on("vector", v_p1)

            def s_p1(scalar, h=h):
                for qt in range(NST):
                    kw = (qt + 1) * 128
                    sM1.wait(scalar, 17 * h + qt + 1)
                    sE1.inc(scalar.activation(
                        pexp[:, 0:kw], ps1[:, 0:kw], AF.Exp,
                        bias=ncol[:, qt:qt + 1],
                        accum_out=ncol[:, NST + qt:NST + qt + 1]), 1)
                sE1.inc(scalar.activation(
                    ncol[:, 2 * NST:3 * NST],
                    ncol[:, NST:2 * NST], AF.Ln), 1)
            k.on("scalar", s_p1)

            def v_seed(vector, h=h):
                sE1.wait(vector, 17 * (h + 1))
                if h >= 1:
                    sSD.wait(vector, 6 * (h - 1) + 1)
                sM1.inc(vector.tensor_sub(sdc[:], ncol[:, 0:NST],
                                          ncol[:, 2 * NST:3 * NST]), 1)
            k.on("vector", v_seed)

            def t_seedtr(tensor, h=h):
                sM1.wait(tensor, 17 * h + 17)
                if h >= 1:
                    sAO.wait(tensor, 4 * h)
                sSD.inc(tensor.transpose(pmix[0:NST, 0:128], sdc[:],
                                         cst["identf"][:]), 1)
            k.on("tensor", t_seedtr)

            def s_seedev(scalar, h=h):
                sSD.wait(scalar, 6 * h + 1)
                sSD.inc(scalar.copy(seeds[:], pmix[0:NST, 0:128]), 1)
            k.on("scalar", s_seedev)

            if h == 0 and ATTN_DEBUG:
                def s_dbg(sync):
                    sQE.wait(sync, 24)
                    sDBG.dma(sync, bass.AP(adbg, 0 * 128 * L, [[L, 128], [1, L]]), qT[0][:])
                    sDBG.dma(sync, bass.AP(adbg, 1 * 128 * L, [[L, 128], [1, L]]), kT[0][:])
                    sDBG.dma(sync, bass.AP(adbg, 2 * 128 * L, [[L, 128], [1, L]]), vtr[0][:])
                    sM1.wait(sync, 18)
                    sDBG.dma(sync, bass.AP(adbf, 0 * 128 * L, [[L, 128], [1, 48]]), ncol[:])
                    sDBG.dma(sync, bass.AP(adbf, 0 * 128 * L + 64, [[L, 128], [1, 16]]), sdc[:])
                    sSD.wait(sync, 2)
                    sDBG.dma(sync, bass.AP(adbg, 3 * 128 * L, [[L, 16], [1, 128]]), seeds[:])
                    sWC.wait(sync, 3)
                    sDBG.dma(sync, bass.AP(adbg, 3 * 128 * L + 256, [[L, 128], [1, 1024]]),
                             wqs[0][:, 0:1024])
                    sLN.wait(sync, LND)
                    sDBG.dma(sync, bass.AP(adbf, 2 * 128 * L, [[L, 128], [1, 2048]]),
                             pexp[:, 0:2048]) if False else None
                    sDBG.dma(sync, bass.AP(adbg, 2 * 128 * L + 0, [[L, 128], [1, 2048]]),
                             ln[:, 0:2048]) if False else None
                    sA2.wait(sync, 2)
                    sDBG.dma(sync, bass.AP(adbf, 1 * 128 * L, [[L, 128], [1, 512]]), seedB[:])
                    sDBG.dma(sync, bass.AP(adbf, 1 * 128 * L + 512, [[L, 128], [1, 1024]]),
                             seedBd[:, 0:1024])
                k.on("sync", s_dbg)

            for qc in range(4):
                c = 4 * h + qc
                nkt = NKTQ[qc]
                gbase = BPH * h + BASEQ[qc]

                def t_seedmm(tensor, h=h, qc=qc, c=c):
                    tensor_wait = 6 * h + 2
                    sSD.wait(tensor, tensor_wait)
                    if qc >= 1:
                        sA2.wait(tensor, 2 * c - 1)
                        sAO.wait(tensor, c)
                    la = None
                    for r in range(4):
                        qt = 4 * qc + r
                        la = tensor.matmul(pmix[:, r * 128:(r + 1) * 128],
                                           esl[:, qt * 128:(qt + 1) * 128],
                                           seeds[:], start=(r == 0), stop=(r == 3))
                    sSD.inc(la, 1)
                k.on("tensor", t_seedmm)

                def s_seedB(scalar, h=h, qc=qc, c=c, gbase=gbase):
                    sSD.wait(scalar, 6 * h + 3 + qc)
                    if c >= 1:
                        sF2.wait(scalar, gbase)
                    sA2.inc(scalar.copy(seedB[:], pmix[:]), 1)
                k.on("scalar", s_seedB)

                def v_seedBd(vector, h=h, qc=qc, c=c):
                    sA2.wait(vector, 2 * c + 1)
                    la = None
                    for r in range(4):
                        la = vector.tensor_add(seedBd[:, r * 512:(r + 1) * 512],
                                               seedB[:],
                                               mdg[:, r * 512:(r + 1) * 512])
                    sA2.inc(la, 1)
                k.on("vector", v_seedBd)

                def t_p2(tensor, h=h, qc=qc, nkt=nkt, gbase=gbase):
                    for kt in range(nkt):
                        g = gbase + kt
                        if g >= 1:
                            sF2.wait(tensor, g)
                        sS2.inc(tensor.matmul(
                            ps2[:], kT[h % 2][:, kt * 128:(kt + 1) * 128],
                            qT[h % 2][:, qc * 512:qc * 512 + 512],
                            start=True, stop=True), 1)
                k.on("tensor", t_p2)

                def v_p2(vector, h=h, qc=qc, c=c, nkt=nkt, gbase=gbase):
                    for kt in range(nkt):
                        g = gbase + kt
                        sS2.wait(vector, g + 1)
                        diag = kt >= 4 * qc
                        if kt == 0:
                            sA2.wait(vector, 2 * c + 2)
                        if g >= 2:
                            sP2.wait(vector, g - 1)
                        srcb = (seedBd[:, (kt - 4 * qc) * 512:(kt - 4 * qc + 1) * 512]
                                if diag else seedB[:])
                        sF2.inc(vector.tensor_add(s2f[g % 2][:], ps2[:], srcb), 1)
                k.on("vector", v_p2)

                def s_p2(scalar, h=h, qc=qc, c=c, nkt=nkt, gbase=gbase):
                    for kt in range(nkt):
                        g = gbase + kt
                        sF2.wait(scalar, g + 1)
                        if kt == 0:
                            sPV.wait(scalar, c)
                        sP2.inc(scalar.activation(
                            pT[:, kt * 512:(kt + 1) * 512], s2f[g % 2][:],
                            AF.Exp), 1)
                k.on("scalar", s_p2)

                def t_pv(tensor, h=h, qc=qc, c=c, nkt=nkt, gbase=gbase):
                    sP2.wait(tensor, gbase + nkt)
                    sAO.wait(tensor, c)
                    la = None
                    for kt in range(nkt):
                        la = tensor.matmul(pmix[:],
                                           vtr[h % 2][:, kt * 128:(kt + 1) * 128],
                                           pT[:, kt * 512:(kt + 1) * 512],
                                           start=(kt == 0), stop=(kt == nkt - 1))
                    sPV.inc(la, 1)
                k.on("tensor", t_pv)

                def s_aoev(scalar, h=h, qc=qc, c=c):
                    sPV.wait(scalar, c + 1)
                    if qc == 0 and h >= 2:
                        sST.wait(scalar, 16 * (h - 1))
                    sAO.inc(scalar.activation(
                        ao[h % 2][:, qc * 512:qc * 512 + 512], pmix[:],
                        AF.Identity, bias=bvs[:, h:h + 1]), 1)
                k.on("scalar", s_aoev)
            if h + 2 < NHEAD:
                k.on("vector", mk_v_wc(h + 2))

        k.entry = [sST, sAO]


_NC_CACHE = {}


def _get_nc(upto=6, taps=()):
    key = (upto, tuple(taps))
    if key not in _NC_CACHE:
        _NC_CACHE[key] = build_program(upto, taps)
    return _NC_CACHE[key]


def make_inputs(ins, b):
    pi = _pi(H)
    rho = _pi(IFF)
    sig = np.concatenate([rho, IFF + rho])

    f32 = np.float32
    x = np.ascontiguousarray(ins["L0_attn_a_x"][b].T[pi]).astype(f32, copy=False)

    def t(a):
        return np.ascontiguousarray(np.asarray(a).T)

    w_in_s = np.ascontiguousarray(t(ins["L0_attn_a_W_in_states"])[pi])
    w_in_g = np.ascontiguousarray(t(ins["L0_attn_a_W_in_gates"])[pi])
    w_dt_T = t(ins["L0_attn_a_W_dt"])
    w_bc_T = np.ascontiguousarray(
        np.concatenate([t(ins["L0_attn_a_W_b"]), t(ins["L0_attn_a_W_c"])], axis=1))
    wdt_r = np.ascontiguousarray(
        w_dt_T.reshape(NDT, 128, RDT).transpose(1, 0, 2).reshape(128, NDT * RDT))
    wbc_r = np.ascontiguousarray(
        w_bc_T.reshape(NDT, 128, 2 * NST).transpose(1, 0, 2).reshape(128, NDT * 2 * NST))
    w_dtp_T = t(ins["L0_attn_a_W_dtproj"])
    w_out_T = np.ascontiguousarray(t(ins["L0_attn_a_W_out"])[:, pi])

    def rearr_cols(v, w):
        return np.ascontiguousarray(
            np.asarray(v).reshape(NDT, 128, w).transpose(1, 0, 2).reshape(128, NDT * w))

    conv_w_r = rearr_cols(ins["L0_attn_a_conv_w"], KCV)
    conv_b_r = rearr_cols(ins["L0_attn_a_conv_b"], 1)
    b_dtp_r = rearr_cols(ins["L0_attn_a_b_dtproj"], 1)
    a_log_r = rearr_cols(ins["L0_attn_a_A_log"], NST)
    d_r = rearr_cols(ins["L0_attn_a_D"], 1)

    wq_T = np.ascontiguousarray(t(ins["L0_attn_b_Wq"])[pi])
    wk_T = np.ascontiguousarray(t(ins["L0_attn_b_Wk"])[pi])
    wv_T = np.ascontiguousarray(t(ins["L0_attn_b_Wv"])[pi])
    wo_T = np.ascontiguousarray(t(ins["L0_attn_b_Wo"])[:, pi])

    def hcols(v):
        return np.ascontiguousarray(
            np.asarray(v).reshape(NHT, 128).T)

    bq_r = hcols(ins["L0_attn_b_bq"])
    bk_r = hcols(ins["L0_attn_b_bk"])
    bv_r = hcols(ins["L0_attn_b_bv"])
    bo_r = hcols(np.asarray(ins["L0_attn_b_bo"])[pi])

    gup = np.asarray(ins["L0_ffn_gate_up_packed"])
    dnp = np.asarray(ins["L0_ffn_down_packed"])
    gub = np.ascontiguousarray(gup.view(np.uint8).T[:, sig])
    dnb = np.ascontiguousarray(dnp.view(np.uint8).T[:, pi])
    gu_scl = np.ascontiguousarray(np.asarray(ins["L0_ffn_gate_up_scales"])[:, sig])
    dn_scl = np.ascontiguousarray(np.asarray(ins["L0_ffn_down_scales"])[:, pi])

    e_gu = np.zeros((NST, 8 * 128), dtype=np.float16)
    for tb in range(8):
        for p in range(128):
            e_gu[2 * tb + (p >= 64), tb * 128 + p] = 1
    e_dn = np.zeros((NIT, 22 * 128), dtype=np.float16)
    for tb in range(22):
        for p in range(128):
            e_dn[2 * tb + (p >= 64), tb * 128 + p] = 1

    e_sel = np.zeros((NST, NST * 128), dtype=BF)
    for r in range(NST):
        e_sel[r, r * 128:(r + 1) * 128] = 1

    mdiag = np.zeros((128, 4 * 512), dtype=np.float32)
    for r in range(4):
        for p in range(128):
            jj = np.arange(512)
            mdiag[p, r * 512:(r + 1) * 512] = np.where(jj >= 128 * r + p, 0.0, NEG)
    mdiag = mdiag.astype(BF)
    jj = np.arange(128)
    mdiag1 = np.where(jj[None, :] <= np.arange(128)[:, None], 0.0,
                      NEG).astype(np.float32)

    return dict(
        x_t=x, w_in_s=w_in_s, w_in_g=w_in_g, wdt_r=wdt_r, wbc_r=wbc_r,
        w_dtp_T=w_dtp_T, w_out_T=w_out_T, conv_w_r=conv_w_r, conv_b_r=conv_b_r,
        b_dtp_r=b_dtp_r, a_log_r=a_log_r, d_r=d_r,
        wq_T=wq_T, wk_T=wk_T, wv_T=wv_T, wo_T=wo_T,
        bq_r=bq_r, bk_r=bk_r, bv_r=bv_r, bo_r=bo_r,
        gub=gub, dnb=dnb, gu_scl=gu_scl, dn_scl=dn_scl,
        e_gu=e_gu, e_dn=e_dn,
        ones1=np.ones((1, 128), dtype=BF),
        onesc=np.ones((128, 1), dtype=BF),
        ident=np.eye(128, dtype=BF),
        identf=np.eye(128, dtype=np.float32),
        e_sel=e_sel, mdiag=mdiag, mdiag1=mdiag1,
        cfc=np.ascontiguousarray(
            np.broadcast_to(np.array(CFCV + [0.0] * (16 - len(CFCV)),
                                     dtype=np.float32), (128, 16))),
    )


def kernel(**inputs):
    nc = _get_nc(upto=6)
    pi = _pi(H)
    in_maps = [make_inputs(inputs, b) for b in range(8)]
    res = run_bass_kernel_spmd(nc, in_maps, core_ids=list(range(8)))
    out = np.empty((8, L, H), dtype=np.float32)
    for b in range(8):
        out[b][:, pi] = np.asarray(res.results[b]["out_t"]).T
    return out


if __name__ == "__main__":
    pass


def build_attn_only(taps=()):
    nc = bass.Bass("TRN2", target_bir_lowering=False, debug=False)

    def dram(name, shape, dt, kind="ExternalInput"):
        return nc.dram_tensor(name, shape, dt, kind=kind)
    xr2 = dram("xr2", [H, L], F32)
    wq_T = dram("wq_T", [H, H], F32)
    wk_T = dram("wk_T", [H, H], F32)
    wv_T = dram("wv_T", [H, H], F32)
    bq_r = dram("bq_r", [128, NHT], F32)
    bk_r = dram("bk_r", [128, NHT], F32)
    bv_r = dram("bv_r", [128, NHT], F32)
    ones1 = dram("ones1", [1, 128], BF16)
    onesc = dram("onesc", [128, 1], BF16)
    ident = dram("ident", [128, 128], BF16)
    identf = dram("identf", [128, 128], F32)
    e_sel = dram("e_sel", [NST, NST * 128], BF16)
    mdiag = dram("mdiag", [128, 4 * 512], BF16)
    mdiag1 = dram("mdiag1", [128, 128], F32)
    cfc = dram("cfc", [128, 16], F32)
    aout_s = nc.dram_tensor("aout_s", [H, L], BF16,
                            kind="ExternalOutput" if "aout_s" in taps else "Internal")
    with contextlib.ExitStack() as ctx:
        blk = ctx.enter_context(nc.Block())
        k = K(nc, blk, ctx)
        cE = ctx.enter_context
        cst = {
            "ones1": cE(nc.sbuf_tensor("c_ones1", [1, 128], BF16)),
            "onesc": cE(nc.sbuf_tensor("c_onesc", [128, 1], BF16)),
            "ident": cE(nc.sbuf_tensor("c_ident", [128, 128], BF16)),
            "identf": cE(nc.sbuf_tensor("c_identf", [128, 128], F32)),
            "cfc": cE(nc.sbuf_tensor("c_cfc", [128, 16], F32)),
        }
        for ci, v in enumerate(CFCV):
            nc.const_aps.register(F32, float(v), cst["cfc"][:, ci:ci + 1])
        sld = k.sem("sld")

        def load_consts(sync):
            for key, srcd in [("ones1", ones1), ("onesc", onesc),
                              ("ident", ident), ("identf", identf),
                              ("cfc", cfc)]:
                sld.dma(sync, cst[key][:], srcd.ap())
        k.on("sync", load_consts)
        k.entry = [sld]
        phase_attn(k, cst, xr2, wq_T, wk_T, wv_T, bq_r, bk_r, bv_r,
                   e_sel, mdiag, mdiag1, aout_s)
        fin = [(s, s.n) for s in k.sems.values() if s.n > 0]

        def final_wait(g, fin=fin):
            for s, v in fin:
                s.wait(g, v)
        k.on("gpsimd", final_wait)
        k.check()
    return nc


# version 13
# speedup vs baseline: 1.0312x; 1.0312x over previous
import contextlib
import numpy as np
import ml_dtypes
import concourse.bass as bass
import concourse.mybir as mybir
from concourse.bass_utils import run_bass_kernel_spmd

F32 = mybir.dt.float32
BF16 = mybir.dt.bfloat16
U8 = mybir.dt.uint8
F16 = mybir.dt.float16
AF = mybir.ActivationFunctionType
ALU = mybir.AluOpType
BF = ml_dtypes.bfloat16

H = 2048
L = 2048
DI = 4096
NST = 16
RDT = 128
KCV = 4
IFF = 5632
NHEAD = 16
DH = 128
NEG = -30000.0
NHT = H // 128
NDT = DI // 128
NIT = IFF // 128
LNEPS = 1e-5
RMSEPS = 1e-6
SCAT = float(DH) ** -0.25
CFCV = [-2.0, -6.0, 2.0, -1.0, 1.0, LNEPS, RMSEPS, 1.0 / H, -0.5, SCAT, 0.0]


def _pi(n):
    return np.concatenate([np.arange(0, n, 2), np.arange(1, n, 2)])


class Sem:
    def __init__(self, ctx, nc, name):
        self.h = ctx.enter_context(nc.semaphore(name))
        self.name = name
        self.n = 0
        self.maxw = 0

    def inc(self, instr, k=1):
        instr.then_inc(self.h, k)
        self.n += k
        return instr

    def dma(self, eng, out, in_):
        self.wait(eng, self.n)
        return self.inc(eng.dma_start(out, in_), 16)

    def dman(self, eng, out, in_):
        return self.inc(eng.dma_start(out, in_), 16)

    def wait(self, eng, val=None):
        if val is None:
            val = self.n
        if val <= 0:
            return
        self.maxw = max(self.maxw, val)
        eng.wait_ge(self.h, val)


class K:
    def __init__(self, nc, blk, ctx):
        self.nc, self.blk, self.ctx = nc, blk, ctx
        self.sems = {}
        self.entry = []

    def sem(self, name):
        if name not in self.sems:
            self.sems[name] = Sem(self.ctx, self.nc, name)
        return self.sems[name]

    def on(self, eng, fn):
        getattr(self.blk, eng)(fn)

    def phase_entry(self, engines=("sync", "gpsimd", "tensor", "scalar", "vector")):
        gate = [(s, s.n) for s in self.entry if s.n > 0]
        for e in engines:
            def f(eng, gate=gate):
                for s, v in gate:
                    s.wait(eng, v)
            self.on(e, f)

    def check(self):
        for s in self.sems.values():
            assert s.maxw <= s.n, f"sem {s.name}: wait {s.maxw} > total {s.n}"


def xap(t, row0, l0, lw, stride=L):
    return bass.AP(t, row0 * stride + l0, [[stride, 128], [1, lw]])


def build_program(upto=6, taps=()):
    nc = bass.Bass("TRN2", target_bir_lowering=False, debug=False)
    nc.detect_race_conditions = False

    def dram(name, shape, dt, kind="ExternalInput"):
        return nc.dram_tensor(name, shape, dt, kind=kind)

    def scratch(name, shape, dt):
        kind = "ExternalOutput" if name in taps else "Internal"
        return nc.dram_tensor(name, shape, dt, kind=kind)

    x_t = dram("x_t", [H, L], F32)
    w_in_s = dram("w_in_s", [H, DI], F32)
    w_in_g = dram("w_in_g", [H, DI], F32)
    wdt_r = dram("wdt_r", [128, NDT * RDT], F32)
    wbc_r = dram("wbc_r", [128, NDT * 2 * NST], F32)
    w_dtp_T = dram("w_dtp_T", [RDT, DI], F32)
    w_out_T = dram("w_out_T", [DI, H], F32)
    conv_w_r = dram("conv_w_r", [128, NDT * KCV], F32)
    conv_b_r = dram("conv_b_r", [128, NDT], F32)
    b_dtp_r = dram("b_dtp_r", [128, NDT], F32)
    a_log_r = dram("a_log_r", [128, NDT * NST], F32)
    d_r = dram("d_r", [128, NDT], F32)
    wq_T = dram("wq_T", [H, H], F32)
    wk_T = dram("wk_T", [H, H], F32)
    wv_T = dram("wv_T", [H, H], F32)
    wo_T = dram("wo_T", [H, H], F32)
    bq_r = dram("bq_r", [128, NHT], F32)
    bk_r = dram("bk_r", [128, NHT], F32)
    bv_r = dram("bv_r", [128, NHT], F32)
    bo_r = dram("bo_r", [128, NHT], F32)
    gub = dram("gub", [H // 2, 2 * IFF], U8)
    dnb = dram("dnb", [IFF // 2, H], U8)
    gu_scl = dram("gu_scl", [NST, 2 * IFF], F16)
    dn_scl = dram("dn_scl", [NIT, H], F16)
    e_gu = dram("e_gu", [NST, 8 * 128], F16)
    e_dn = dram("e_dn", [NIT, 22 * 128], F16)
    ones1 = dram("ones1", [1, 128], BF16)
    onesc = dram("onesc", [128, 1], BF16)
    ident = dram("ident", [128, 128], BF16)
    identf = dram("identf", [128, 128], F32)
    e_sel = dram("e_sel", [NST, NST * 128], BF16)
    mdiag = dram("mdiag", [128, 4 * 512], BF16)
    mdiag1 = dram("mdiag1", [128, 128], F32)
    cfc = dram("cfc", [128, 16], F32)

    out_t = dram("out_t", [H, L], F32, kind="ExternalOutput")

    u_s = scratch("u_s", [DI, L], BF16)
    sg_s = scratch("sg_s", [DI, L], BF16)
    y2_s = scratch("y2_s", [DI, L], BF16)
    bc_s = scratch("bc_s", [2 * NST, L], BF16)
    ts_s = scratch("ts_s", [RDT, L], BF16)
    ypart_s = scratch("ypart_s", [DI, L], F32)
    aout_s = scratch("aout_s", [H, L], BF16)
    xr1 = scratch("xr1", [H, L], F32)
    xr2 = scratch("xr2", [H, L], F32)
    xr3 = scratch("xr3", [H, L], F32)
    wgu_s = scratch("wgu_s", [H, 2 * IFF], BF16)
    wdn_s = scratch("wdn_s", [IFF, H], BF16)

    with contextlib.ExitStack() as ctx:
        blk = ctx.enter_context(nc.Block())
        k = K(nc, blk, ctx)
        cE = ctx.enter_context
        cst = {
            "ones1": cE(nc.sbuf_tensor("c_ones1", [1, 128], BF16)),
            "onesc": cE(nc.sbuf_tensor("c_onesc", [128, 1], BF16)),
            "ident": cE(nc.sbuf_tensor("c_ident", [128, 128], BF16)),
            "identf": cE(nc.sbuf_tensor("c_identf", [128, 128], F32)),
            "cfc": cE(nc.sbuf_tensor("c_cfc", [128, 16], F32)),
        }
        cfcv = cst["cfc"]
        for ci, v in enumerate(CFCV):
            nc.const_aps.register(F32, float(v), cfcv[:, ci:ci + 1])
        sld = k.sem("dq_in")

        def load_consts(sync):
            for key, src in [("ones1", ones1), ("onesc", onesc),
                             ("ident", ident), ("identf", identf),
                             ("cfc", cfc)]:
                sld.dma(sync, cst[key][:], src.ap())
        k.on("sync", load_consts)
        k.entry = [sld]

        if upto == 0:
            phase_dequant(k, cst, gub, dnb, gu_scl, dn_scl, e_gu, e_dn,
                          wgu_s, wdn_s)
        else:
            phase_mamba_a(k, cst, x_t, w_in_s, w_in_g, wdt_r, wbc_r,
                          conv_w_r, conv_b_r, u_s, sg_s, bc_s, ts_s,
                          dq_args=(gub, dnb, gu_scl, dn_scl, e_gu, e_dn,
                                   wgu_s, wdn_s))
        if upto >= 2:
            phase_ssm(k, cst, u_s, sg_s, bc_s, ts_s, w_dtp_T, b_dtp_r,
                      a_log_r, d_r, ypart_s, y2_s)
        if upto >= 3:
            phase_proj_res(k, cst, "p3", w_out_T, DI, y2_s, x_t,
                           xr1 if upto > 3 else out_t, bias_r=None)
        if upto >= 4:
            phase_ffn(k, cst, "f1", xr1, wgu_s, wdn_s,
                      xr2 if upto > 4 else out_t)
        if upto >= 5:
            phase_attn(k, cst, xr2, wq_T, wk_T, wv_T, bq_r, bk_r, bv_r,
                       e_sel, mdiag, mdiag1, aout_s)
            phase_proj_res(k, cst, "wo", wo_T, H, aout_s, xr2,
                           xr3 if upto > 5 else out_t, bias_r=bo_r)
        if upto >= 6:
            phase_ffn(k, cst, "f2", xr3, wgu_s, wdn_s, out_t)

        fin = [(s, s.n) for s in k.sems.values() if s.n > 0]

        def final_wait(g, fin=fin):
            for s, v in fin:
                s.wait(g, v)
        k.on("gpsimd", final_wait)
        k.check()
    return nc


def phase_dequant(k, cst, gub, dnb, gu_scl, dn_scl, e_gu, e_dn, wgu_s, wdn_s,
                  defer=False, E=None, g_pre=None, g_hook=None):
    nc = k.nc
    sDQI = k.sem("dq_i")
    sDQS = k.sem("dq_s")
    sV = k.sem("dq_v")

    OC = 1024
    units = [("gu", tb, c) for tb in range(8) for c in range(11)] + \
            [("dn", tb, c) for tb in range(22) for c in range(2)]

    def uinfo(u):
        which, tb, c = u
        if which == "gu":
            return which, tb, c * 1024, 1024, 2 * IFF, gub, wgu_s, H // 2
        return which, tb, c * 1024, 1024, H, dnb, wdn_s, IFF // 2

    ctx = None
    if E is None:
        ctx = contextlib.ExitStack()
        E = ctx.enter_context
    byt = [E(nc.sbuf_tensor(f"dq_byt{i}", [128, OC], U8)) for i in range(2)]
    scl = [E(nc.sbuf_tensor(f"dq_scl{i}", [128, OC], BF16)) for i in range(2)]
    xb = E(nc.sbuf_tensor("dq_x", [128, OC], U8))
    s8 = E(nc.sbuf_tensor("dq_s8", [128, OC], U8))
    a1 = E(nc.sbuf_tensor("dq_a1", [128, OC], BF16))
    sgn = E(nc.sbuf_tensor("dq_sgn", [128, OC], BF16))
    r1 = E(nc.sbuf_tensor("dq_r1", [128, OC], BF16))
    r2 = E(nc.sbuf_tensor("dq_r2", [128, OC], BF16))
    wv = [E(nc.sbuf_tensor(f"dq_wv{i}", [128, OC], BF16)) for i in range(2)]

    def in_load(g, ui):
        if ui >= len(units):
            return
        which, tb, o0, oc, O, src, outd, hr = uinfo(units[ui])
        sscl = gu_scl if which == "gu" else dn_scl
        SW = 2 * IFF if which == "gu" else H
        if ui >= 2:
            sV.wait(g, 4 * (ui - 2) + 3)
        sDQI.dman(g, byt[ui % 2][:, 0:oc],
                 bass.AP(src, tb * 128 * O + o0, [[O, 128], [1, oc]]))
        if ui >= 2:
            sV.wait(g, 4 * (ui - 2) + 4)
        sDQI.dman(g, scl[ui % 2][0:64, 0:oc],
                 bass.AP(sscl, (2 * tb) * SW + o0, [[0, 64], [1, oc]]))
        sDQI.dman(g, scl[ui % 2][64:128, 0:oc],
                 bass.AP(sscl, (2 * tb + 1) * SW + o0, [[0, 64], [1, oc]]))

    def g_io(g):
        if g_pre is not None:
            g_pre(g)
        in_load(g, 0)
        in_load(g, 1)
        for ui, u in enumerate(units):
            if g_hook is not None:
                g_hook(g, ui)
            which, tb, o0, oc, O, src, outd, hr = uinfo(u)
            sV.wait(g, 4 * ui + 2)
            sDQS.dman(g, 
                bass.AP(outd, tb * 128 * O + o0, [[O, 128], [1, oc]]),
                wv[0][:, 0:oc])
            in_load(g, ui + 2)
            sV.wait(g, 4 * ui + 4)
            sDQS.dman(g, 
                bass.AP(outd, (hr + tb * 128) * O + o0, [[O, 128], [1, oc]]),
                wv[1][:, 0:oc])

    def act_unit(scalar, ui):
        pass

    def dve_unit(vector, ui):
        if ui >= len(units):
            return
        which, tb, o0, oc, O, src, outd, hr = uinfo(units[ui])
        sDQI.wait(vector, 48 * ui + 48)
        b = byt[ui % 2][:, 0:oc]
        for p in range(2):
            si = 2 * ui + p
            if si >= 2:
                sDQS.wait(vector, 16 * (si - 1))
            wvb = wv[p][:, 0:oc]
            if p == 0:
                vector.tensor_single_scalar(xb[:, 0:oc], b, 7,
                                            op=ALU.bitwise_and)
                vector.tensor_single_scalar(s8[:, 0:oc], b, 8,
                                            op=ALU.bitwise_and)
            else:
                vector.tensor_scalar(xb[:, 0:oc], b, 4, 7,
                                     op0=ALU.logical_shift_right,
                                     op1=ALU.bitwise_and)
                vector.tensor_scalar(s8[:, 0:oc], b, 4, 8,
                                     op0=ALU.logical_shift_right,
                                     op1=ALU.bitwise_and)
            vector.tensor_scalar_mul(a1[:, 0:oc], xb[:, 0:oc], 0.5)
            sV.inc(vector.tensor_scalar(sgn[:, 0:oc], s8[:, 0:oc],
                                        -0.25, 1.0,
                                        op0=ALU.mult, op1=ALU.add), 1)
            vector.tensor_scalar(r1[:, 0:oc], a1[:, 0:oc], -2.0, 0.0,
                                 op0=ALU.add, op1=ALU.max)
            vector.tensor_scalar(r2[:, 0:oc], a1[:, 0:oc], -3.0, 0.0,
                                 op0=ALU.add, op1=ALU.max)
            vector.tensor_add(wvb, a1[:, 0:oc], r1[:, 0:oc])
            vector.scalar_tensor_tensor(wvb, r2[:, 0:oc],
                                        cst["cfc"][:, 2:3], wvb,
                                        op0=ALU.mult, op1=ALU.add)
            vector.tensor_mul(wvb, wvb, sgn[:, 0:oc])
            sV.inc(vector.tensor_mul(wvb, wvb,
                                     scl[ui % 2][:, 0:oc]), 1)

    if defer:
        k.on("gpsimd", g_io)
        return dict(dve=dve_unit, act=act_unit, n=len(units), sDQS=sDQS)

    with ctx:
        k.phase_entry()
        k.on("gpsimd", g_io)

        def v_main(vector):
            for ui in range(len(units)):
                dve_unit(vector, ui)
        k.on("vector", v_main)

        k.entry = [sDQS]


def emit_ln(k, cst, tag, x_src, l0, lw, lnb, bufs, base):
    nc = k.nc
    sX, sSTT, sROW, sBC, sLN = (k.sem(tag + s) for s in
                                ("_x", "_stat", "_row", "_bc", "_ln"))
    xin, xsq, rows, rbf, sB, msB, pstatB, pbc = bufs
    rwA, rwB, rwC = rows
    rbA, rbB = rbf
    pstatA = pbc
    X0, ST0, R0, BC0, LN0 = base
    ones1, onesc = cst["ones1"], cst["onesc"]
    nj = lw // 512

    def s_xload(sync):
        for i in range(NHT):
            if i >= 2:
                sSTT.wait(sync, ST0 + i - 1)
            sX.dma(sync, xin[i % 2][:, 0:lw],
                                  xap(x_src, i * 128, l0, lw))
    k.on("sync", s_xload)

    def s_sq(scalar):
        for i in range(NHT):
            sX.wait(scalar, X0 + 16 * (i + 1))
            if i >= 2:
                sSTT.wait(scalar, ST0 + i - 1)
            sROW.inc(scalar.activation(xsq[i % 2][:, 0:lw], xin[i % 2][:, 0:lw],
                                       AF.Square), 1)
    k.on("scalar", s_sq)

    def v_xbf(vector):
        for i in range(NHT):
            sX.wait(vector, X0 + 16 * (i + 1))
            sLN.inc(vector.tensor_copy(lnb[:, i * lw:(i + 1) * lw],
                                       xin[i % 2][:, 0:lw]), 1)
    k.on("vector", v_xbf)

    def t_stats(tensor):
        for i in range(NHT):
            sLN.wait(tensor, LN0 + i + 1)
            sROW.wait(tensor, R0 + i + 1)
            la = None
            for j in range(nj):
                tensor.matmul(pstatA[0:1, j * 512:(j + 1) * 512], onesc[:],
                              lnb[:, i * lw + j * 512:i * lw + (j + 1) * 512],
                              start=(i == 0), stop=(i == NHT - 1))
            for j in range(nj):
                la = tensor.matmul(pstatB[0:1, j * 512:(j + 1) * 512], onesc[:],
                                   xsq[i % 2][:, j * 512:(j + 1) * 512],
                                   start=(i == 0), stop=(i == NHT - 1))
            sSTT.inc(la, 1)
    k.on("tensor", t_stats)

    def s_rows(scalar):
        sSTT.wait(scalar, ST0 + NHT)
        scalar.copy(rwA, pstatA[0:1, 0:lw])
        sROW.inc(scalar.copy(rwB, pstatB[0:1, 0:lw]), 1)
    k.on("scalar", s_rows)
    RR = R0 + NHT + 1

    def v_rows(vector):
        sROW.wait(vector, RR)
        vector.tensor_scalar_mul(rwA, rwA, 1.0 / H)
        vector.tensor_mul(rwC, rwA, rwA)
        sROW.inc(vector.scalar_tensor_tensor(rwB, rwB, 1.0 / H, rwC,
                                             op0=ALU.mult, op1=ALU.subtract), 1)
    k.on("vector", v_rows)

    def s_rstd(scalar):
        sROW.wait(scalar, RR + 1)
        scalar.activation(rwC, rwB, AF.Ln, bias=LNEPS)
        sROW.inc(scalar.activation(rwB, rwC, AF.Exp, scale=-0.5), 1)
    k.on("scalar", s_rstd)

    def v_rows2(vector):
        sROW.wait(vector, RR + 2)
        vector.tensor_mul(rwC, rwA, rwB)
        vector.tensor_copy(rbA, rwB)
        sROW.inc(vector.tensor_copy(rbB, rwC), 1)
    k.on("vector", v_rows2)

    def t_bc(tensor):
        sROW.wait(tensor, RR + 3)
        la = None
        for j in range(nj):
            la = tensor.matmul(pbc[:, j * 512:(j + 1) * 512], ones1[:],
                               rbA[0:1, j * 512:(j + 1) * 512],
                               start=True, stop=True)
        sSTT.inc(la, 1)
        sBC.wait(tensor, BC0 + 1)
        for j in range(nj):
            la = tensor.matmul(pbc[:, j * 512:(j + 1) * 512], ones1[:],
                               rbB[0:1, j * 512:(j + 1) * 512],
                               start=True, stop=True)
        sSTT.inc(la, 1)
    k.on("tensor", t_bc)

    def s_bcev(scalar):
        sSTT.wait(scalar, ST0 + NHT + 1)
        sBC.inc(scalar.copy(sB[:, 0:lw], pbc[:, 0:lw]), 1)
        sSTT.wait(scalar, ST0 + NHT + 2)
        sBC.inc(scalar.copy(msB[:, 0:lw], pbc[:, 0:lw]), 1)
    k.on("scalar", s_bcev)

    def v_ln(vector):
        sBC.wait(vector, BC0 + 2)
        for i in range(NHT):
            t = lnb[:, i * lw:(i + 1) * lw]
            vector.tensor_mul(t, t, sB[:, 0:lw])
            sLN.inc(vector.tensor_sub(t, t, msB[:, 0:lw]), 1)
    k.on("vector", v_ln)


def phase_mamba_a(k, cst, x_t, w_in_s, w_in_g, wdt_r, wbc_r,
                  conv_w_r, conv_b_r, u_s, sg_s, bc_s, ts_s, dq_args=None):
    nc = k.nc
    sLN = k.sem("a_ln")
    sPRE = k.sem("a_pre")
    sW = k.sem("a_w")
    sWC = k.sem("a_wc")
    sPC = k.sem("a_pc")
    sMM = k.sem("a_mm")
    sHS = k.sem("a_hs")
    sSG = k.sem("a_sg")
    sCV = k.sem("a_cv")
    sSIL = k.sem("a_sil")
    sTS = k.sem("a_ts")
    sOUT = k.sem("a_out")

    with contextlib.ExitStack() as ctx:
        E = ctx.enter_context
        ln = E(nc.sbuf_tensor("a_lnf", [128, NHT * L], BF16))
        hsp = E(nc.sbuf_tensor("a_hsp", [128, L + KCV - 1], F32))
        acc = E(nc.sbuf_tensor("a_acc", [128, L], F32))
        u_t = [E(nc.sbuf_tensor(f"a_u{i}", [128, L], BF16)) for i in range(2)]
        sg_t = [E(nc.sbuf_tensor(f"a_sg{i}", [128, L], BF16)) for i in range(2)]
        wsA = [E(nc.sbuf_tensor(f"a_wsA{i}", [128, NHT * 128], BF16)) for i in range(2)]
        wsG = [E(nc.sbuf_tensor(f"a_wsG{i}", [128, NHT * 128], BF16)) for i in range(2)]
        cw = E(nc.sbuf_tensor("a_cw", [128, NDT * KCV], F32))
        cb = E(nc.sbuf_tensor("a_cb", [128, NDT], F32))
        wdt = E(nc.sbuf_tensor("a_wdt", [128, NDT * 128], BF16))
        wbc = E(nc.sbuf_tensor("a_wbc", [128, NDT * 32], BF16))
        ts_sb = E(nc.sbuf_tensor("a_ts", [128, L], F32))
        ts_bf = E(nc.sbuf_tensor("a_tsbf", [128, L], BF16))
        bc_sb = E(nc.sbuf_tensor("a_bc", [32, L], F32))
        bcbf = E(nc.sbuf_tensor("a_bcbf", [32, L], BF16))
        k.phase_entry()
        assert dq_args is not None

        def g_strip(g, c):
            if c >= 2 * NDT:
                return
            t, j = c // 2, c % 2
            wsrc = w_in_s if j == 0 else w_in_g
            dstt = wsA[t % 2] if j == 0 else wsG[t % 2]
            if t >= 2:
                sMM.wait(g, 12 * (t - 2) + 8)
            sW.dman(g, dstt[:].rearrange("p (t o) -> p t o", t=NHT),
                    bass.AP(wsrc, t * 128,
                            [[DI, 128], [128 * DI, NHT], [1, 128]]))

        def g_pre(g):
            sW.dman(g, wdt[:], wdt_r.ap())
            sW.dman(g, wbc[:], wbc_r.ap())
            for c in range(4):
                g_strip(g, c)

        DQN = 8 * 11 + 22 * 2
        ui_of_t = [DQN * t // NDT for t in range(NDT)]

        def g_hook(g, ui):
            for t in range(NDT):
                if ui_of_t[t] == ui and (t == 0 or ui_of_t[t - 1] != ui):
                    g_strip(g, 2 * t + 4)
                    g_strip(g, 2 * t + 5)

        dq = phase_dequant(k, cst, *dq_args, defer=True, E=E,
                           g_pre=g_pre, g_hook=g_hook)
        assert dq["n"] == DQN

        def dqu(t):
            return range(dq["n"] * t // NDT, dq["n"] * (t + 1) // NDT)

        with contextlib.ExitStack() as lctx:
            lE = lctx.enter_context
            xin = [lE(nc.sbuf_tensor(f"a_xin{i}", [128, L], F32)) for i in range(2)]
            xsq = [lE(nc.sbuf_tensor(f"a_xsq{i}", [128, L], BF16)) for i in range(2)]
            sB = u_t[0]
            msB = u_t[1]
            pstatB = lE(nc.psum_tensor("a_pstB", [1, L], F32))
            pbc0 = lE(nc.psum_tensor("a_pbc", [128, L], F32))
            emit_ln(k, cst, "a", x_t, 0, L, ln,
                    (xin, xsq,
                     (acc[0:1, 0:L], hsp[0:1, 0:L], ts_sb[0:1, 0:L]),
                     (u_t[0][0:1, 0:L], u_t[1][0:1, 0:L]),
                     sB, msB, pstatB, pbc0),
                    (0, 0, 0, 0, 0))
        LND = 2 * NHT

        psA = [E(nc.psum_tensor(f"a_psA{i}", [128, 512], F32)) for i in range(2)]
        psG = [E(nc.psum_tensor(f"a_psG{i}", [128, 512], F32)) for i in range(2)]
        pTS = E(nc.psum_tensor("a_pTS", [128, 512], F32))
        pBC = E(nc.psum_tensor("a_pBC", [32, 512], F32))

        def s_io(sync):
            sPRE.dma(sync, cw[:], conv_w_r.ap())
            sPRE.dma(sync, cb[:], conv_b_r.ap())
            for t in range(NDT):
                sSG.wait(sync, 4 * (t + 1))
                sOUT.dma(sync, xap(sg_s, t * 128, 0, L),
                                        sg_t[t % 2][:])
                sSIL.wait(sync, t + 1)
                sOUT.dma(sync, xap(u_s, t * 128, 0, L),
                                        u_t[t % 2][:])
            sTS.wait(sync, 4 * NDT + 2)
            sOUT.dma(sync, bc_s.ap(), bcbf[:])
            sOUT.dma(sync, ts_s.ap(), ts_bf[:])
        k.on("sync", s_io)

        def t_inproj(tensor):
            sLN.wait(tensor, LND)
            sW.wait(tensor, 32)
            for t in range(NDT):
                sW.wait(tensor, 64 + 32 * t)
                for lc in range(4):
                    g = 4 * t + lc
                    if g >= 2:
                        sHS.wait(tensor, g - 1)
                    la = None
                    for kk in range(NHT):
                        la = tensor.matmul(
                            psA[lc % 2][:],
                            wsA[t % 2][:, kk * 128:(kk + 1) * 128],
                            ln[:, kk * L + lc * 512:kk * L + lc * 512 + 512],
                            start=(kk == 0), stop=(kk == NHT - 1))
                    sMM.inc(la, 1)
                    if g >= 2:
                        sSG.wait(tensor, g - 1)
                    for kk in range(NHT):
                        la = tensor.matmul(
                            psG[lc % 2][:],
                            wsG[t % 2][:, kk * 128:(kk + 1) * 128],
                            ln[:, kk * L + lc * 512:kk * L + lc * 512 + 512],
                            start=(kk == 0), stop=(kk == NHT - 1))
                    sMM.inc(la, 1)
                sSIL.wait(tensor, t + 1)
                for lc in range(4):
                    sTS.wait(tensor, 4 * t + lc)
                    la = tensor.matmul(pTS[:], wdt[:, t * 128:(t + 1) * 128],
                                       u_t[t % 2][:, lc * 512:lc * 512 + 512],
                                       start=True, stop=True)
                    la = tensor.matmul(pBC[:], wbc[:, t * 32:(t + 1) * 32],
                                       u_t[t % 2][:, lc * 512:lc * 512 + 512],
                                       start=True, stop=True)
                    sMM.inc(la, 1)
        k.on("tensor", t_inproj)

        def s_evac(scalar):
            for t in range(NDT):
                for lc in range(4):
                    sMM.wait(scalar, 12 * t + 2 * lc + 1)
                    if lc == 0:
                        sCV.wait(scalar, t)
                    sHS.inc(scalar.copy(hsp[:, 3 + lc * 512:3 + lc * 512 + 512],
                                        psA[lc % 2][:]), 1)
                    sMM.wait(scalar, 12 * t + 2 * lc + 2)
                    if lc == 0 and t >= 2:
                        sOUT.wait(scalar, 32 * (t - 1))
                    sSG.inc(scalar.activation(
                        sg_t[t % 2][:, lc * 512:lc * 512 + 512],
                        psG[lc % 2][:], AF.Silu), 1)
                sCV.wait(scalar, t + 1)
                if t >= 2:
                    sMM.wait(scalar, 12 * (t - 1))
                sSIL.inc(scalar.activation(u_t[t % 2][:], acc[:], AF.Silu), 1)
        k.on("scalar", s_evac)

        def v_conv(vector):
            sPRE.wait(vector, 32)
            for t in range(NDT):
                sHS.wait(vector, 4 * (t + 1))
                if t >= 1:
                    sSIL.wait(vector, t)
                vector.memset(hsp[:, 0:KCV - 1], 0.0)
                vector.tensor_scalar(acc[:], hsp[:, 0:L],
                                     cw[:, KCV * t:KCV * t + 1], cb[:, t:t + 1],
                                     op0=ALU.mult, op1=ALU.add)
                last = None
                for j in range(1, KCV):
                    last = vector.scalar_tensor_tensor(
                        acc[:], hsp[:, j:j + L],
                        cw[:, KCV * t + j:KCV * t + j + 1],
                        acc[:], op0=ALU.mult, op1=ALU.add)
                sCV.inc(last, 1)
                if t == 0:
                    vector.memset(ts_sb[:], 0.0)
                    vector.memset(bc_sb[:], 0.0)
                for lc in range(4):
                    sMM.wait(vector, 12 * t + 8 + lc + 1)
                    vector.tensor_add(ts_sb[:, lc * 512:lc * 512 + 512], pTS[:],
                                      ts_sb[:, lc * 512:lc * 512 + 512])
                    sTS.inc(vector.tensor_add(
                        bc_sb[:, lc * 512:lc * 512 + 512], pBC[:],
                        bc_sb[:, lc * 512:lc * 512 + 512]), 1)
                for ui in dqu(t):
                    dq["dve"](vector, ui)
            vector.tensor_copy(ts_bf[:], ts_sb[:])
            sTS.inc(vector.tensor_copy(bcbf[:], bc_sb[:]), 2)
        k.on("vector", v_conv)


        k.entry = [sOUT, sMM, sCV] + ([dq["sDQS"]] if dq else [])


def phase_ssm(k, cst, u_s, sg_s, bc_s, ts_s, w_dtp_T, b_dtp_r, a_log_r, d_r,
              ypart_s, y2_s):
    nc = k.nc
    sPRE = k.sem("s_pre")
    sBC = k.sem("s_bcin")
    sU = k.sem("s_u")
    sDTM = k.sem("s_dtm")
    sZ = k.sem("s_z")
    sSP = k.sem("s_sp")
    sDT = k.sem("s_dt")
    sDU = k.sem("s_du")
    sDA = k.sem("s_da")
    sSCN = k.sem("s_scn")
    sPE = k.sem("s_pe")
    sYE = k.sem("s_ye")
    sST = k.sem("s_st")

    with contextlib.ExitStack() as ctx:
        E = ctx.enter_context
        wdtp = E(nc.sbuf_tensor("s_wdtp", [128, DI], BF16))
        bdtp = E(nc.sbuf_tensor("s_bdtp", [128, NDT], F32))
        Asb = E(nc.sbuf_tensor("s_A", [128, NDT * NST], F32))
        Dsb = E(nc.sbuf_tensor("s_D", [128, NDT], F32))
        tsb = E(nc.sbuf_tensor("s_ts", [128, L], BF16))
        bcast = E(nc.sbuf_tensor("s_bcast", [128, 8 * L], BF16))
        ccast = E(nc.sbuf_tensor("s_ccast", [128, 8 * L], BF16))
        u_t = [E(nc.sbuf_tensor(f"s_u{i}", [128, L], BF16)) for i in range(2)]
        sg_t = [E(nc.sbuf_tensor(f"s_sg{i}", [128, L], BF16)) for i in range(2)]
        yp_t = [E(nc.sbuf_tensor(f"s_yp{i}", [128, L], F32)) for i in range(2)]
        z = E(nc.sbuf_tensor("s_z", [128, L], F32))
        tmp = E(nc.sbuf_tensor("s_tmp", [128, L], F32))
        rl = E(nc.sbuf_tensor("s_rl", [128, L], F32))
        dt = E(nc.sbuf_tensor("s_dt", [128, L], F32))
        du = E(nc.sbuf_tensor("s_du", [128, L], BF16))
        dA = [E(nc.sbuf_tensor(f"s_dA{i}", [128, L], F32)) for i in range(2)]
        dBu = E(nc.sbuf_tensor("s_dBu", [128, L], BF16))
        hh = E(nc.sbuf_tensor("s_h", [128, L], BF16))
        ch = E(nc.sbuf_tensor("s_ch", [128, L], BF16))
        t0 = E(nc.sbuf_tensor("s_t0", [128, L], F32))
        yo = E(nc.sbuf_tensor("s_yo", [128, L], F32))
        y2o = E(nc.sbuf_tensor("s_y2o", [128, L], BF16))
        pDT = [E(nc.psum_tensor(f"s_pDT{i}", [128, 512], F32)) for i in range(2)]
        pY = E(nc.psum_tensor("s_pY", [128, L], F32))
        k.phase_entry()

        def s_pre(sync):
            sPRE.dma(sync, 
                z[:], bass.AP(w_dtp_T, 0, [[DI, 128], [1, 2048]]))
            sPRE.dma(sync, 
                tmp[:], bass.AP(w_dtp_T, 2048, [[DI, 128], [1, 2048]]))
            sPRE.dma(sync, bdtp[:], b_dtp_r.ap())
            sPRE.dma(sync, Asb[:], a_log_r.ap())
            sPRE.dma(sync, Dsb[:], d_r.ap())
            sPRE.dma(sync, tsb[:], ts_s.ap())
        k.on("sync", s_pre)

        def s_A(scalar):
            sPRE.wait(scalar, 96)
            sPRE.inc(scalar.activation(Asb[:], Asb[:], AF.Exp), 1)
        k.on("scalar", s_A)

        def v_A(vector):
            sPRE.wait(vector, 97)
            vector.tensor_scalar_mul(Asb[:], Asb[:], -1.0)
            vector.tensor_copy(wdtp[:, 0:2048], z[:])
            sPRE.inc(vector.tensor_copy(wdtp[:, 2048:4096], tmp[:]), 1)
        k.on("vector", v_A)
        APRE = 98

        for half in range(2):
            UPT = 1 if half == 0 else 3
            U0 = 16 * NDT * half

            def s_io(sync, half=half):
                if half == 1:
                    sSCN.wait(sync, 2 * 8 * NDT)
                for nn in range(8):
                    na = half * 8 + nn
                    sBC.dma(sync, bcast[:, nn * L:(nn + 1) * L],
                                           bass.AP(bc_s, na * L,
                                                   [[0, 128], [1, L]]))
                    sBC.dma(sync, ccast[:, nn * L:(nn + 1) * L],
                                           bass.AP(bc_s, (NST + na) * L,
                                                   [[0, 128], [1, L]]))

                def load(t):
                    if t >= NDT:
                        return
                    it = half * NDT + t
                    if it >= 2:
                        sYE.wait(sync, 3 * (it - 1))
                    sU.dma(sync, u_t[t % 2][:],
                                          xap(u_s, t * 128, 0, L))
                    if half == 1:
                        sU.dma(sync, sg_t[t % 2][:],
                                              xap(sg_s, t * 128, 0, L))
                        sU.dma(sync, yp_t[t % 2][:],
                                              xap(ypart_s, t * 128, 0, L))
                load(0)
                load(1)
                for t in range(NDT):
                    it = half * NDT + t
                    load(t + 2)
                    sYE.wait(sync, 3 * (it + 1))
                    if half == 0:
                        sST.dma(sync, xap(ypart_s, t * 128, 0, L),
                                               yo[:])
                    else:
                        sST.dma(sync, xap(y2_s, t * 128, 0, L),
                                               y2o[:])
            k.on("sync", s_io)
            BC0 = 16 * 16 * (half + 1)

            def t_main(tensor, half=half):
                if half == 0:
                    sPRE.wait(tensor, APRE)
                for t in range(NDT):
                    it = half * NDT + t
                    for lc in range(4):
                        g = 4 * it + lc
                        if g >= 2:
                            sZ.wait(tensor, g - 1)
                        sDTM.inc(tensor.matmul(
                            pDT[lc % 2][:], wdtp[:, t * 128:(t + 1) * 128],
                            tsb[:, lc * 512:lc * 512 + 512],
                            start=True, stop=True), 1)
                    for nn in range(8):
                        gg = 8 * it + nn
                        sSCN.wait(tensor, 2 * gg + 2)
                        la = None
                        for lc in range(4):
                            la = tensor.matmul(pY[:, lc * 512:lc * 512 + 512],
                                               cst["ident"][:],
                                               ch[:, lc * 512:lc * 512 + 512],
                                               start=(nn == 0), stop=(nn == 7))
                        sPE.inc(la, 1)
                    if it < 2 * NDT - 1:
                        sYE.wait(tensor, 3 * it + 1)
            k.on("tensor", t_main)

            def s_act(scalar, half=half):
                for t in range(NDT):
                    it = half * NDT + t
                    for lc in range(4):
                        sDTM.wait(scalar, 4 * it + lc + 1)
                        if lc == 0 and it >= 1:
                            sDA.wait(scalar, 8 * it)
                            sDU.wait(scalar, it)
                            sDT.wait(scalar, it)
                        sZ.inc(scalar.activation(z[:, lc * 512:lc * 512 + 512],
                                                 pDT[lc % 2][:], AF.Identity,
                                                 bias=bdtp[:, t:t + 1]), 1)
                    scalar.activation(tmp[:], z[:], AF.Abs)
                    scalar.activation(tmp[:], tmp[:], AF.Exp, scale=-1.0)
                    sSP.inc(scalar.activation(tmp[:], tmp[:], AF.Ln, bias=1.0), 3)
                    for nn in range(8):
                        na = half * 8 + nn
                        gg = 8 * it + nn
                        sDT.wait(scalar, it + 1)
                        if gg >= 2:
                            sSCN.wait(scalar, 2 * gg - 3)
                        sDA.inc(scalar.activation(
                            dA[nn % 2][:], dt[:], AF.Exp,
                            scale=Asb[:, t * NST + na:t * NST + na + 1]), 1)
            k.on("scalar", s_act)

            def v_main(vector, half=half, U0=U0, UPT=UPT, BC0=BC0):
                for t in range(NDT):
                    it = half * NDT + t
                    sZ.wait(vector, 4 * it + 4)
                    vector.tensor_scalar_max(rl[:], z[:], 0.0)
                    sSP.wait(vector, 3 * (it + 1))
                    sDT.inc(vector.tensor_add(dt[:], tmp[:], rl[:]), 1)
                    sU.wait(vector, U0 + 16 * UPT * t + 16)
                    sDU.inc(vector.tensor_mul(du[:], dt[:], u_t[t % 2][:]), 1)
                    for nn in range(8):
                        gg = 8 * it + nn
                        sDA.wait(vector, gg + 1)
                        if t == 0 and nn == 0:
                            sBC.wait(vector, BC0)
                        vector.tensor_mul(dBu[:], du[:],
                                          bcast[:, nn * L:(nn + 1) * L])
                        sSCN.inc(vector.tensor_tensor_scan(
                            hh[:], dA[nn % 2][:], dBu[:], 0.0,
                            op0=ALU.mult, op1=ALU.add), 1)
                        if gg >= 1:
                            sPE.wait(vector, gg)
                        sSCN.inc(vector.tensor_mul(
                            ch[:], hh[:], ccast[:, nn * L:(nn + 1) * L]), 1)
                    sPE.wait(vector, 8 * (it + 1))
                    if half == 0:
                        if it >= 1:
                            sST.wait(vector, 16 * it)
                        sYE.inc(vector.tensor_copy(yo[:], pY[:]), 3)
                    else:
                        sST.wait(vector, 16 * NDT + 16 * t)
                        sYE.inc(vector.scalar_tensor_tensor(
                            t0[:], u_t[t % 2][:], Dsb[:, t:t + 1], pY[:],
                            op0=ALU.mult, op1=ALU.add), 1)
                        sU.wait(vector, U0 + 48 * t + 48)
                        vector.tensor_add(t0[:], t0[:], yp_t[t % 2][:])
                        sYE.inc(vector.tensor_mul(y2o[:], t0[:],
                                                  sg_t[t % 2][:]), 2)
            k.on("vector", v_main)

        k.entry = [sST, sPE, sSCN]


def phase_proj_res(k, cst, tag, w_T, KD, rhs_s, x_src, dst, bias_r=None):
    nc = k.nc
    NKT = KD // 128
    sW = k.sem(tag + "_w")
    sWC = k.sem(tag + "_wc")
    sMM = k.sem(tag + "_mm")
    sEV = k.sem(tag + "_ev")
    sX = k.sem(tag + "_x")
    sY = k.sem(tag + "_y")
    sST = k.sem(tag + "_st")

    with contextlib.ExitStack() as ctx:
        E = ctx.enter_context
        yfull = E(nc.sbuf_tensor(tag + "_yf", [128, NKT * L], BF16))
        ws = [E(nc.sbuf_tensor(f"{tag}_ws{i}", [128, NKT * 128], BF16))
              for i in range(2)]
        wsf = [E(nc.sbuf_tensor(f"{tag}_wsf{i}", [128, NKT * 128], F32))
               for i in range(2)]
        xo = [E(nc.sbuf_tensor(f"{tag}_xo{i}", [128, L], F32)) for i in range(2)]
        bsb = E(nc.sbuf_tensor(tag + "_b", [128, NHT], F32))
        ps = [E(nc.psum_tensor(f"{tag}_ps{i}", [128, 512], F32)) for i in range(2)]
        k.phase_entry()

        def ws_dma(sync, o):
            if o >= NHT:
                return
            if o >= 2:
                sWC.wait(sync, o - 1)
            sW.dma(sync, 
                wsf[o % 2][:].rearrange("p (t o) -> p t o", t=NKT),
                bass.AP(w_T, o * 128, [[H, 128], [128 * H, NKT], [1, 128]]))

        def s_yl(scalar):
            for t in range(NKT):
                sY.dman(scalar, yfull[:, t * L:(t + 1) * L],
                        xap(rhs_s, t * 128, 0, L))
        k.on("scalar", s_yl)

        def s_io(sync):
            if bias_r is not None:
                sW.dma(sync, bsb[:], bias_r.ap())
            ws_dma(sync, 0)
            ws_dma(sync, 1)
            sX.dma(sync, xo[0][:], xap(x_src, 0, 0, L))
            sX.dma(sync, xo[1][:], xap(x_src, 128, 0, L))
            for o in range(NHT):
                ws_dma(sync, o + 2)
                sEV.wait(sync, 4 * (o + 1))
                sST.dma(sync, xap(dst, o * 128, 0, L), xo[o % 2][:])
                if o + 2 < NHT:
                    sST.wait(sync, 16 * (o + 1))
                    sX.dma(sync, xo[o % 2][:],
                                          xap(x_src, (o + 2) * 128, 0, L))
        k.on("sync", s_io)
        WB = 16 if bias_r is not None else 0

        def t_mm(tensor):
            sY.wait(tensor, 16 * NKT)
            for o in range(NHT):
                sWC.wait(tensor, o + 1)
                for lc in range(4):
                    g = 4 * o + lc
                    if g >= 2:
                        sEV.wait(tensor, g - 1)
                    la = None
                    for kk in range(NKT):
                        la = tensor.matmul(
                            ps[lc % 2][:], ws[o % 2][:, kk * 128:(kk + 1) * 128],
                            yfull[:, kk * L + lc * 512:kk * L + lc * 512 + 512],
                            start=(kk == 0), stop=(kk == NKT - 1))
                    sMM.inc(la, 1)
        k.on("tensor", t_mm)

        def ws_conv(vector, o):
            if o >= NHT:
                return
            sW.wait(vector, WB + 16 * (o + 1))
            if o >= 2:
                sMM.wait(vector, 4 * (o - 1))
            sWC.inc(vector.tensor_copy(ws[o % 2][:], wsf[o % 2][:]), 1)

        def v_ev(vector):
            ws_conv(vector, 0)
            ws_conv(vector, 1)
            for o in range(NHT):
                sX.wait(vector, 16 * (o + 1))
                if o >= 2:
                    sST.wait(vector, 16 * (o - 1))
                for lc in range(4):
                    sMM.wait(vector, 4 * o + lc + 1)
                    sl = slice(lc * 512, lc * 512 + 512)
                    if bias_r is not None:
                        sEV.inc(vector.scalar_tensor_tensor(
                            xo[o % 2][:, sl], ps[lc % 2][:], bsb[:, o:o + 1],
                            xo[o % 2][:, sl], op0=ALU.add, op1=ALU.add), 1)
                    else:
                        sEV.inc(vector.tensor_add(xo[o % 2][:, sl],
                                                  ps[lc % 2][:],
                                                  xo[o % 2][:, sl]), 1)
                ws_conv(vector, o + 2)
        k.on("vector", v_ev)

        def ws_conv(vector, o):
            if o >= NHT:
                return
            sW.wait(vector, WB + 16 * (o + 1))
            if o >= 2:
                sMM.wait(vector, 4 * (o - 1))
            sWC.inc(vector.tensor_copy(ws[o % 2][:], wsf[o % 2][:]), 1)

        def v_ev(vector):
            ws_conv(vector, 0)
            ws_conv(vector, 1)
            for o in range(NHT):
                ws_conv(vector, o + 2)
                sX.wait(vector, 16 * (o + 1))
                if o >= 2:
                    sST.wait(vector, 16 * (o - 1))
        k.entry = [sST, sMM]


def phase_ffn(k, cst, tag, x_src, wgu_s, wdn_s, dst):
    nc = k.nc
    LC = 1024
    sX = k.sem(tag + "_x")
    sSTT = k.sem(tag + "_stat")
    sROW = k.sem(tag + "_row")
    sBC = k.sem(tag + "_bc")
    sLN = k.sem(tag + "_ln")
    sWG = k.sem(tag + "_wg")
    sWD = k.sem(tag + "_wd")
    sMG = k.sem(tag + "_mg")
    sSI = k.sem(tag + "_si")
    sHM = k.sem(tag + "_hm")
    sMD = k.sem(tag + "_md")
    sYE = k.sem(tag + "_ye")
    sRS = k.sem(tag + "_rs")
    sRB = k.sem(tag + "_rb")
    sFN = k.sem(tag + "_fn")
    sST = k.sem(tag + "_st")
    sXO = sX

    ones1, onesc = cst["ones1"], cst["onesc"]

    with contextlib.ExitStack() as ctx:
        E = ctx.enter_context
        lnyb = E(nc.sbuf_tensor(f"{tag}_lnyb", [128, NHT * LC], BF16))
        wg = [E(nc.sbuf_tensor(f"{tag}_wg{i}", [128, NHT * 128], BF16))
              for i in range(3)]
        wd = [E(nc.sbuf_tensor(f"{tag}_wd{i}", [128, NIT * 128], BF16))
              for i in range(2)]
        sg = E(nc.sbuf_tensor(f"{tag}_sg", [128, LC], BF16))
        hi = E(nc.sbuf_tensor(f"{tag}_hi", [128, NIT * LC], BF16))
        ysq = [E(nc.sbuf_tensor(f"{tag}_ysq{i}", [128, LC], BF16))
               for i in range(2)]
        rB = E(nc.sbuf_tensor(f"{tag}_rB", [128, LC], BF16))
        xo = [E(nc.sbuf_tensor(f"{tag}_xo{i}", [128, LC], F32)) for i in range(2)]
        t1 = E(nc.sbuf_tensor(f"{tag}_t1", [128, LC], F32))
        k.phase_entry()

        for lci in range(2):
            l0 = lci * LC
            if lci == 1:
                for e in ("sync", "gpsimd", "tensor", "scalar", "vector"):
                    def f(eng):
                        sST.wait(eng, 16 * NHT)
                    k.on(e, f)

            with contextlib.ExitStack() as lctx:
                lE = lctx.enter_context
                xin = [lE(nc.sbuf_tensor(f"{tag}_xin{lci}_{i}", [128, LC], F32))
                       for i in range(2)]
                xsq = [lE(nc.sbuf_tensor(f"{tag}_xsq{lci}_{i}", [128, LC], BF16))
                       for i in range(2)]
                sB = lE(nc.sbuf_tensor(f"{tag}_sB{lci}", [128, LC], BF16))
                msB = lE(nc.sbuf_tensor(f"{tag}_msB{lci}", [128, LC], BF16))
                pstatB = lE(nc.psum_tensor(f"{tag}_psB{lci}", [1, LC], F32))
                pbc = lE(nc.psum_tensor(f"{tag}_pbc{lci}", [128, LC], F32))
                emit_ln(k, cst, tag, x_src, l0, LC, lnyb,
                        (xin, xsq,
                         (t1[0:1, 0:LC], xo[0][0:1, 0:LC], xo[1][0:1, 0:LC]),
                         (ysq[0][0:1, 0:LC], ysq[1][0:1, 0:LC]),
                         sB, msB, pstatB, pbc),
                        (32 * NHT * lci, (NHT + 2) * lci, (NHT + 5) * lci,
                         3 * lci, 2 * NHT * lci))
            LND = 2 * NHT * lci + 2 * NHT
            RR = (NHT + 5) * lci + NHT + 1

            WG0 = 16 * 2 * NIT * lci
            MG0 = 4 * NIT * lci
            SI0 = NIT * lci
            HM0 = NIT * lci

            with contextlib.ExitStack() as pctx:
                pg = [pctx.enter_context(
                    nc.psum_tensor(f"{tag}_pg{lci}_{i}", [128, LC], F32))
                    for i in range(2)]

                def g_wg(g, lci=lci, MG0=MG0):
                    for i2 in range(2 * NIT):
                        if i2 >= 3:
                            sMG.wait(g, MG0 + 2 * (i2 - 2))
                        pi, gu = i2 // 2, i2 % 2
                        oi = pi + gu * NIT
                        sWG.dma(g, 
                            wg[i2 % 3][:].rearrange("p (t o) -> p t o", t=NHT),
                            bass.AP(wgu_s, oi * 128,
                                    [[2 * IFF, 128], [128 * 2 * IFF, NHT],
                                     [1, 128]]))
                k.on("gpsimd", g_wg)

                def t_gu(tensor, LND=LND, WG0=WG0, SI0=SI0, HM0=HM0):
                    sLN.wait(tensor, LND)
                    for pi in range(NIT):
                        for gu in range(2):
                            i2 = 2 * pi + gu
                            sWG.wait(tensor, WG0 + 16 * (i2 + 1))
                            if gu == 0 and pi >= 1:
                                sSI.wait(tensor, SI0 + pi)
                            if gu == 1 and pi >= 1:
                                sHM.wait(tensor, HM0 + pi)
                            la = None
                            for j in range(2):
                                for kk in range(NHT):
                                    la = tensor.matmul(
                                        pg[gu][:, j * 512:(j + 1) * 512],
                                        wg[i2 % 3][:, kk * 128:(kk + 1) * 128],
                                        lnyb[:, kk * LC + j * 512:kk * LC + j * 512 + 512],
                                        start=(kk == 0), stop=(kk == NHT - 1))
                            sMG.inc(la, 2)
                    sSI.wait(tensor, SI0 + NIT)
                    sHM.wait(tensor, HM0 + NIT)
                k.on("tensor", t_gu)

                def s_silu(scalar, MG0=MG0, HM0=HM0):
                    for pi in range(NIT):
                        sMG.wait(scalar, MG0 + 4 * pi + 2)
                        if pi >= 1:
                            sHM.wait(scalar, HM0 + pi)
                        sSI.inc(scalar.activation(sg[:], pg[0][:], AF.Silu), 1)
                k.on("scalar", s_silu)

                def v_h(vector, lci=lci, MG0=MG0, SI0=SI0):
                    if lci == 1:
                        sMD.wait(vector, NHT)
                    for pi in range(NIT):
                        sMG.wait(vector, MG0 + 4 * pi + 4)
                        sSI.wait(vector, SI0 + pi + 1)
                        sHM.inc(vector.tensor_mul(hi[:, pi * LC:(pi + 1) * LC],
                                                  pg[1][:], sg[:]), 1)
                k.on("vector", v_h)

            WD0 = 16 * NHT * lci
            MD0 = NHT * lci
            YE0 = 2 * NHT * lci
            RS0 = 17 * lci

            with contextlib.ExitStack() as pctx:
                pd = [pctx.enter_context(
                    nc.psum_tensor(f"{tag}_pd{lci}_{i}", [128, LC], F32))
                    for i in range(2)]
                prs = pctx.enter_context(
                    nc.psum_tensor(f"{tag}_prs{lci}", [1, LC], F32))
                pbc2 = pctx.enter_context(
                    nc.psum_tensor(f"{tag}_pbc2{lci}", [128, LC], F32))

                def g_wd(g, lci=lci, MD0=MD0):
                    for ho in range(NHT):
                        if ho >= 2:
                            sMD.wait(g, MD0 + ho - 1)
                        sWD.dma(g, 
                            wd[ho % 2][:].rearrange("p (t o) -> p t o", t=NIT),
                            bass.AP(wdn_s, ho * 128,
                                    [[H, 128], [128 * H, NIT], [1, 128]]))
                k.on("gpsimd", g_wd)

                def t_dn(tensor, lci=lci, WD0=WD0, MD0=MD0, YE0=YE0, RR=RR):
                    sHM.wait(tensor, NIT * lci + NIT)
                    for ho in range(NHT):
                        sWD.wait(tensor, WD0 + 16 * (ho + 1))
                        if ho >= 2:
                            sYE.wait(tensor, YE0 + 2 * (ho - 1))
                        la = None
                        for j in range(2):
                            for kk in range(NIT):
                                la = tensor.matmul(
                                    pd[ho % 2][:, j * 512:(j + 1) * 512],
                                    wd[ho % 2][:, kk * 128:(kk + 1) * 128],
                                    hi[:, kk * LC + j * 512:kk * LC + j * 512 + 512],
                                    start=(kk == 0), stop=(kk == NIT - 1))
                        sMD.inc(la, 1)
                        if ho >= 1:
                            sYE.wait(tensor, YE0 + 2 * ho)
                            la = None
                            for j in range(2):
                                la = tensor.matmul(
                                    prs[0:1, j * 512:(j + 1) * 512], onesc[:],
                                    ysq[(ho - 1) % 2][:, j * 512:(j + 1) * 512],
                                    start=(ho == 1), stop=False)
                            sRS.inc(la, 1)
                    sYE.wait(tensor, YE0 + 2 * NHT)
                    la = None
                    for j in range(2):
                        la = tensor.matmul(prs[0:1, j * 512:(j + 1) * 512],
                                           onesc[:],
                                           ysq[(NHT - 1) % 2][:, j * 512:(j + 1) * 512],
                                           start=False, stop=True)
                    sRS.inc(la, 1)
                    sROW.wait(tensor, RR + 4)
                    sBC.wait(tensor, 3 * lci + 3)
                    la = None
                    for j in range(2):
                        la = tensor.matmul(pbc2[:, j * 512:(j + 1) * 512],
                                           ones1[:],
                                           ysq[0][0:1, j * 512:(j + 1) * 512],
                                           start=True, stop=True)
                    sRS.inc(la, 1)
                k.on("tensor", t_dn)

                def s_yev(scalar, lci=lci, MD0=MD0, RS0=RS0, YE0=YE0):
                    for ho in range(NHT):
                        sMD.wait(scalar, MD0 + ho + 1)
                        if ho >= 2:
                            sRS.wait(scalar, RS0 + ho - 1)
                        if lci == 1 and ho < 2:
                            sFN.wait(scalar, NHT)
                        scalar.activation(ysq[ho % 2][:], pd[ho % 2][:], AF.Square)
                        sYE.inc(scalar.copy(lnyb[:, ho * LC:(ho + 1) * LC],
                                            pd[ho % 2][:]), 2)
                    sRS.wait(scalar, RS0 + NHT)
                    rr = t1[0:1, 0:LC]
                    scalar.copy(rr, prs[0:1, 0:LC])
                    scalar.activation(rr, rr, AF.Ln, scale=1.0 / H, bias=RMSEPS)
                    sROW.inc(scalar.activation(rr, rr, AF.Exp, scale=-0.5), 1)
                    sRS.wait(scalar, RS0 + NHT + 1)
                    sRB.inc(scalar.copy(rB[:], pbc2[:, 0:LC]), 1)
                k.on("scalar", s_yev)

                def v_rinv(vector, RR=RR):
                    sROW.wait(vector, RR + 4)
                    sBC.inc(vector.tensor_copy(ysq[0][0:1, 0:LC],
                                               t1[0:1, 0:LC]), 1)
                k.on("vector", v_rinv)

                def s_xio(sync, lci=lci, l0=l0, RR=RR):
                    sROW.wait(sync, RR + 3)
                    sXO.dma(sync, xo[0][:],
                                           xap(x_src, 0, l0, LC))
                    sXO.dma(sync, xo[1][:],
                                           xap(x_src, 128, l0, LC))
                    for ho in range(NHT):
                        sFN.wait(sync, lci * NHT + ho + 1)
                        sST.dma(sync, xap(dst, ho * 128, l0, LC),
                                               xo[ho % 2][:])
                        if ho + 2 < NHT:
                            sST.wait(sync, 16 * (lci * NHT + ho + 1))
                            sXO.dma(sync, 
                                xo[ho % 2][:],
                                xap(x_src, (ho + 2) * 128, l0, LC))
                k.on("sync", s_xio)
                XO0 = 32 * NHT * lci + 16 * NHT

                def v_fin(vector, lci=lci, XO0=XO0, YE0=YE0):
                    sRB.wait(vector, lci + 1)
                    sYE.wait(vector, YE0 + 2 * NHT)
                    for ho in range(NHT):
                        sXO.wait(vector, XO0 + 16 * (ho + 1))
                        if ho >= 2:
                            sST.wait(vector, 16 * (lci * NHT + ho - 1))
                        vector.tensor_mul(t1[:], lnyb[:, ho * LC:(ho + 1) * LC],
                                          rB[:])
                        sFN.inc(vector.tensor_add(xo[ho % 2][:], t1[:],
                                                  xo[ho % 2][:]), 1)
                k.on("vector", v_fin)

        k.entry = [sST, sFN]


ATTN_DEBUG = False


def phase_attn(k, cst, x_src, wq_T, wk_T, wv_T, bq_r, bk_r, bv_r,
               e_sel, mdiag, mdiag1, aout_s):
    nc = k.nc
    adbg = nc.dram_tensor("adbg", [4 * 128, L], BF16,
                          kind="ExternalOutput" if ATTN_DEBUG else "Internal")
    adbf = nc.dram_tensor("adbf", [4 * 128, L], F32,
                          kind="ExternalOutput" if ATTN_DEBUG else "Internal")
    sDBG = k.sem("at_dbg") if ATTN_DEBUG else None
    sLN = k.sem("at_ln")
    sPB = k.sem("at_pb")
    sW = k.sem("at_w")
    sWC = k.sem("at_wc")
    sQK = k.sem("at_qk")
    sQE = k.sem("at_qe")
    sS1 = k.sem("at_s1")
    sM1 = k.sem("at_m1")
    sE1 = k.sem("at_e1")
    sSD = k.sem("at_sd")
    sS2 = k.sem("at_s2")
    sA2 = k.sem("at_a2")
    sF2 = k.sem("at_f2")
    sP2 = k.sem("at_p2")
    sPV = k.sem("at_pv")
    sAO = k.sem("at_ao")
    sST = k.sem("at_st")

    ones1, onesc = cst["ones1"], cst["onesc"]
    SC = float(DH) ** -0.25
    NKTQ = [4 * qc + 4 for qc in range(4)]
    BASEQ = [sum(NKTQ[:qc]) for qc in range(4)]
    BPH = sum(NKTQ)

    with contextlib.ExitStack() as ctx:
        E = ctx.enter_context
        ln = E(nc.sbuf_tensor("at_lnf", [128, NHT * L], BF16))
        esl = E(nc.sbuf_tensor("at_esl", [NST, NST * 128], BF16))
        mdg = E(nc.sbuf_tensor("at_mdg", [128, 4 * 512], BF16))
        mdg1 = E(nc.sbuf_tensor("at_mdg1", [128, 128], F32))
        wqs = [E(nc.sbuf_tensor(f"at_wqs{i}", [128, NHT * 128], BF16))
               for i in range(2)]
        wks = [E(nc.sbuf_tensor(f"at_wks{i}", [128, NHT * 128], BF16))
               for i in range(2)]
        wvs = [E(nc.sbuf_tensor(f"at_wvs{i}", [128, NHT * 128], BF16))
               for i in range(2)]
        wsf = [E(nc.sbuf_tensor(f"at_wsf{i}", [128, NHT * 128], F32))
               for i in range(2)]
        bqs = E(nc.sbuf_tensor("at_bqs", [128, NHT], F32))
        bks = E(nc.sbuf_tensor("at_bks", [128, NHT], F32))
        bvs = E(nc.sbuf_tensor("at_bvs", [128, NHT], F32))
        pexp = E(nc.sbuf_tensor("at_pexp", [128, L], BF16))
        seedBd = E(nc.sbuf_tensor("at_seedBd", [128, 4 * 512], F32))
        pT = E(nc.sbuf_tensor("at_pT", [128, NST * 512], BF16))
        k.phase_entry()

        with contextlib.ExitStack() as lctx:
            lE = lctx.enter_context
            xin = [lE(nc.sbuf_tensor(f"at_xin{i}", [128, L], F32)) for i in range(2)]
            xsq = [lE(nc.sbuf_tensor(f"at_xsq{i}", [128, L], BF16)) for i in range(2)]
            sB = lE(nc.sbuf_tensor("at_sB", [128, L], BF16))
            msB = lE(nc.sbuf_tensor("at_msB", [128, L], BF16))
            pstatB = lE(nc.psum_tensor("at_psB", [1, L], F32))
            pbc = lE(nc.psum_tensor("at_pbc", [128, L], F32))
            emit_ln(k, cst, "at", x_src, 0, L, ln,
                    (xin, xsq,
                     (wsf[0][0:1, 0:L], wsf[1][0:1, 0:L], seedBd[0:1, 0:L]),
                     (pexp[0:1, 0:L], pT[0:1, 0:L]),
                     sB, msB, pstatB, pbc),
                    (0, 0, 0, 0, 0))
        LND = 2 * NHT

        qT = [E(nc.sbuf_tensor(f"at_qT{i}", [128, L], BF16)) for i in range(2)]
        kT = [E(nc.sbuf_tensor(f"at_kT{i}", [128, L], BF16)) for i in range(2)]
        vtr = [E(nc.sbuf_tensor(f"at_vtr{i}", [128, L], BF16)) for i in range(2)]
        ncol = E(nc.sbuf_tensor("at_ncol", [128, 3 * NST], F32))
        sdc = E(nc.sbuf_tensor("at_sdc", [128, NST], F32))
        seeds = E(nc.sbuf_tensor("at_seeds", [NST, 128], BF16))
        seedB = E(nc.sbuf_tensor("at_seedB", [128, 512], F32))
        s2f = [E(nc.sbuf_tensor(f"at_s2f{i}", [128, 512], F32)) for i in range(2)]
        ao = [E(nc.sbuf_tensor(f"at_ao{i}", [128, L], BF16)) for i in range(2)]

        pqkv = [E(nc.psum_tensor(f"at_pqkv{i}", [128, 512], F32)) for i in range(2)]
        ps1 = E(nc.psum_tensor("at_ps1", [128, L], F32))
        ps2 = E(nc.psum_tensor("at_ps2", [128, 512], F32))
        pmix = E(nc.psum_tensor("at_pmix", [128, 512], F32))

        sROW = k.sem("at_row")

        def strip_dma(sync, d):
            if d >= 3 * NHEAD:
                return
            h, wi = d // 3, d % 3
            if d < 4:
                sROW.wait(sync, NHT + 4)
            if d >= 2:
                sWC.wait(sync, d - 1)
            wsrc = [wq_T, wk_T, wv_T][wi]
            sW.dma(sync, 
                wsf[d % 2][:].rearrange("p (t o) -> p t o", t=NHT),
                bass.AP(wsrc, h * 128, [[H, 128], [128 * H, NHT], [1, 128]]))

        def s_io(sync):
            sPB.dma(sync, esl[:], e_sel.ap())
            sPB.dma(sync, mdg[:], mdiag.ap())
            sPB.dma(sync, mdg1[:], mdiag1.ap())
            for d, s in [(bq_r, bqs), (bk_r, bks), (bv_r, bvs)]:
                sPB.dma(sync, s[:], d.ap())
            for d in range(4):
                strip_dma(sync, d)
            for h in range(NHEAD):
                for wi in range(3):
                    strip_dma(sync, 3 * h + 4 + wi)
                sAO.wait(sync, 4 * (h + 1))
                sST.dma(sync, xap(aout_s, h * 128, 0, L), ao[h % 2][:])
        k.on("sync", s_io)

        def v_bs(vector):
            sPB.wait(vector, 96)
            vector.tensor_scalar_mul(bqs[:], bqs[:], SC)
            sPB.inc(vector.tensor_scalar_mul(bks[:], bks[:], SC), 1)
        k.on("vector", v_bs)
        PB0 = 97

        def mk_t_qkv(h):
            def t_qkv(tensor, h=h):
                if h == 0:
                    sLN.wait(tensor, LND)
                    sPB.wait(tensor, PB0)
                sWC.wait(tensor, 3 * (h + 1))
                if h >= 2:
                    sAO.wait(tensor, 4 * (h - 1))
                def evac_wait(g):
                    if g >= 2:
                        sQE.wait(tensor, g - 1)
                for wi in range(2):
                    ws = [wqs, wks][wi]
                    for lc in range(4):
                        g = 24 * h + 4 * wi + lc
                        evac_wait(g)
                        la = None
                        for kk in range(NHT):
                            la = tensor.matmul(
                                pqkv[g % 2][:],
                                ws[h % 2][:, kk * 128:(kk + 1) * 128],
                                ln[:, kk * L + lc * 512:kk * L + lc * 512 + 512],
                                start=(kk == 0), stop=(kk == NHT - 1))
                        sQK.inc(la, 1)
                for lt in range(NHT):
                    g = 24 * h + 8 + lt
                    evac_wait(g)
                    la = None
                    for kk in range(NHT):
                        la = tensor.matmul(
                            pqkv[g % 2][:, 0:128],
                            ln[:, kk * L + lt * 128:kk * L + lt * 128 + 128],
                            wvs[h % 2][:, kk * 128:(kk + 1) * 128],
                            start=(kk == 0), stop=(kk == NHT - 1))
                    sQK.inc(la, 1)
            return t_qkv
        k.on("tensor", mk_t_qkv(0))

        def mk_v_wc(h):
            def v_wc(vector, h=h):
                for wi in range(3):
                    d = 3 * h + wi
                    sW.wait(vector, 16 * (d + 1))
                    dst = [wqs, wks, wvs][wi][h % 2]
                    if h >= 2:
                        sQK.wait(vector, 24 * (h - 1))
                    sWC.inc(vector.tensor_copy(dst[:], wsf[d % 2][:]), 1)
            return v_wc
        k.on("vector", mk_v_wc(0))
        k.on("vector", mk_v_wc(1))

        for h in range(NHEAD):
            if h >= 1:
                k.on("tensor", mk_t_qkv(h))

            def s_qkv_ev(scalar, h=h):
                for wi, (dstt, bb, sc) in enumerate([(qT, bqs, SC),
                                                     (kT, bks, SC)]):
                    for lc in range(4):
                        g = 24 * h + 4 * wi + lc
                        sQK.wait(scalar, g + 1)
                        sQE.inc(scalar.activation(
                            dstt[h % 2][:, lc * 512:lc * 512 + 512],
                            pqkv[g % 2][:], AF.Identity, bias=bb[:, h:h + 1],
                            scale=sc), 1)
                for lt in range(NHT):
                    g = 24 * h + 8 + lt
                    sQK.wait(scalar, g + 1)
                    sQE.inc(scalar.copy(vtr[h % 2][:, lt * 128:(lt + 1) * 128],
                                        pqkv[g % 2][:, 0:128]), 1)
            k.on("scalar", s_qkv_ev)

            def t_p1(tensor, h=h):
                for qt in range(NST):
                    kw = (qt + 1) * 128
                    if qt == 0:
                        sQE.wait(tensor, 24 * h + 8)
                    if qt >= 1:
                        sE1.wait(tensor, 17 * h + qt)
                    la = None
                    for kc in range((kw + 511) // 512):
                        cw_ = min(512, kw - kc * 512)
                        la = tensor.matmul(ps1[:, kc * 512:kc * 512 + cw_],
                                           qT[h % 2][:, qt * 128:(qt + 1) * 128],
                                           kT[h % 2][:, kc * 512:kc * 512 + cw_],
                                           start=True, stop=True)
                    sS1.inc(la, 1)
            k.on("tensor", t_p1)

            def v_p1(vector, h=h):
                for qt in range(NST):
                    kw = (qt + 1) * 128
                    sS1.wait(vector, NST * h + qt + 1)
                    if qt == 0 and h >= 1:
                        sE1.wait(vector, 17 * h)
                    vector.tensor_add(ps1[:, qt * 128:kw],
                                      ps1[:, qt * 128:kw], mdg1[:])
                    sM1.inc(vector.tensor_reduce(
                        ncol[:, qt:qt + 1], ps1[:, 0:kw],
                        axis=mybir.AxisListType.X, op=ALU.max, negate=True), 1)
            k.on("vector", v_p1)

            def s_p1(scalar, h=h):
                for qt in range(NST):
                    kw = (qt + 1) * 128
                    sM1.wait(scalar, 17 * h + qt + 1)
                    sE1.inc(scalar.activation(
                        pexp[:, 0:kw], ps1[:, 0:kw], AF.Exp,
                        bias=ncol[:, qt:qt + 1],
                        accum_out=ncol[:, NST + qt:NST + qt + 1]), 1)
                sE1.inc(scalar.activation(
                    ncol[:, 2 * NST:3 * NST],
                    ncol[:, NST:2 * NST], AF.Ln), 1)
            k.on("scalar", s_p1)

            def v_seed(vector, h=h):
                sE1.wait(vector, 17 * (h + 1))
                if h >= 1:
                    sSD.wait(vector, 6 * (h - 1) + 1)
                sM1.inc(vector.tensor_sub(sdc[:], ncol[:, 0:NST],
                                          ncol[:, 2 * NST:3 * NST]), 1)
            k.on("vector", v_seed)

            def t_seedtr(tensor, h=h):
                sM1.wait(tensor, 17 * h + 17)
                if h >= 1:
                    sAO.wait(tensor, 4 * h)
                sSD.inc(tensor.transpose(pmix[0:NST, 0:128], sdc[:],
                                         cst["identf"][:]), 1)
            k.on("tensor", t_seedtr)

            def s_seedev(scalar, h=h):
                sSD.wait(scalar, 6 * h + 1)
                sSD.inc(scalar.copy(seeds[:], pmix[0:NST, 0:128]), 1)
            k.on("scalar", s_seedev)

            if h == 0 and ATTN_DEBUG:
                def s_dbg(sync):
                    sQE.wait(sync, 24)
                    sDBG.dma(sync, bass.AP(adbg, 0 * 128 * L, [[L, 128], [1, L]]), qT[0][:])
                    sDBG.dma(sync, bass.AP(adbg, 1 * 128 * L, [[L, 128], [1, L]]), kT[0][:])
                    sDBG.dma(sync, bass.AP(adbg, 2 * 128 * L, [[L, 128], [1, L]]), vtr[0][:])
                    sM1.wait(sync, 18)
                    sDBG.dma(sync, bass.AP(adbf, 0 * 128 * L, [[L, 128], [1, 48]]), ncol[:])
                    sDBG.dma(sync, bass.AP(adbf, 0 * 128 * L + 64, [[L, 128], [1, 16]]), sdc[:])
                    sSD.wait(sync, 2)
                    sDBG.dma(sync, bass.AP(adbg, 3 * 128 * L, [[L, 16], [1, 128]]), seeds[:])
                    sWC.wait(sync, 3)
                    sDBG.dma(sync, bass.AP(adbg, 3 * 128 * L + 256, [[L, 128], [1, 1024]]),
                             wqs[0][:, 0:1024])
                    sLN.wait(sync, LND)
                    sDBG.dma(sync, bass.AP(adbf, 2 * 128 * L, [[L, 128], [1, 2048]]),
                             pexp[:, 0:2048]) if False else None
                    sDBG.dma(sync, bass.AP(adbg, 2 * 128 * L + 0, [[L, 128], [1, 2048]]),
                             ln[:, 0:2048]) if False else None
                    sA2.wait(sync, 2)
                    sDBG.dma(sync, bass.AP(adbf, 1 * 128 * L, [[L, 128], [1, 512]]), seedB[:])
                    sDBG.dma(sync, bass.AP(adbf, 1 * 128 * L + 512, [[L, 128], [1, 1024]]),
                             seedBd[:, 0:1024])
                k.on("sync", s_dbg)

            for qc in range(4):
                c = 4 * h + qc
                nkt = NKTQ[qc]
                gbase = BPH * h + BASEQ[qc]

                def t_seedmm(tensor, h=h, qc=qc, c=c):
                    tensor_wait = 6 * h + 2
                    sSD.wait(tensor, tensor_wait)
                    if qc >= 1:
                        sA2.wait(tensor, 2 * c - 1)
                        sAO.wait(tensor, c)
                    la = None
                    for r in range(4):
                        qt = 4 * qc + r
                        la = tensor.matmul(pmix[:, r * 128:(r + 1) * 128],
                                           esl[:, qt * 128:(qt + 1) * 128],
                                           seeds[:], start=(r == 0), stop=(r == 3))
                    sSD.inc(la, 1)
                k.on("tensor", t_seedmm)

                def s_seedB(scalar, h=h, qc=qc, c=c, gbase=gbase):
                    sSD.wait(scalar, 6 * h + 3 + qc)
                    if c >= 1:
                        sF2.wait(scalar, gbase)
                    sA2.inc(scalar.copy(seedB[:], pmix[:]), 1)
                k.on("scalar", s_seedB)

                def v_seedBd(vector, h=h, qc=qc, c=c):
                    sA2.wait(vector, 2 * c + 1)
                    la = None
                    for r in range(4):
                        la = vector.tensor_add(seedBd[:, r * 512:(r + 1) * 512],
                                               seedB[:],
                                               mdg[:, r * 512:(r + 1) * 512])
                    sA2.inc(la, 1)
                k.on("vector", v_seedBd)

                def t_p2(tensor, h=h, qc=qc, nkt=nkt, gbase=gbase):
                    for kt in range(nkt):
                        g = gbase + kt
                        if g >= 1:
                            sF2.wait(tensor, g)
                        sS2.inc(tensor.matmul(
                            ps2[:], kT[h % 2][:, kt * 128:(kt + 1) * 128],
                            qT[h % 2][:, qc * 512:qc * 512 + 512],
                            start=True, stop=True), 1)
                k.on("tensor", t_p2)

                def v_p2(vector, h=h, qc=qc, c=c, nkt=nkt, gbase=gbase):
                    for kt in range(nkt):
                        g = gbase + kt
                        sS2.wait(vector, g + 1)
                        diag = kt >= 4 * qc
                        if kt == 0:
                            sA2.wait(vector, 2 * c + 2)
                        if g >= 2:
                            sP2.wait(vector, g - 1)
                        srcb = (seedBd[:, (kt - 4 * qc) * 512:(kt - 4 * qc + 1) * 512]
                                if diag else seedB[:])
                        sF2.inc(vector.tensor_add(s2f[g % 2][:], ps2[:], srcb), 1)
                k.on("vector", v_p2)

                def s_p2(scalar, h=h, qc=qc, c=c, nkt=nkt, gbase=gbase):
                    for kt in range(nkt):
                        g = gbase + kt
                        sF2.wait(scalar, g + 1)
                        if kt == 0:
                            sPV.wait(scalar, c)
                        sP2.inc(scalar.activation(
                            pT[:, kt * 512:(kt + 1) * 512], s2f[g % 2][:],
                            AF.Exp), 1)
                k.on("scalar", s_p2)

                def t_pv(tensor, h=h, qc=qc, c=c, nkt=nkt, gbase=gbase):
                    sP2.wait(tensor, gbase + nkt)
                    sAO.wait(tensor, c)
                    la = None
                    for kt in range(nkt):
                        la = tensor.matmul(pmix[:],
                                           vtr[h % 2][:, kt * 128:(kt + 1) * 128],
                                           pT[:, kt * 512:(kt + 1) * 512],
                                           start=(kt == 0), stop=(kt == nkt - 1))
                    sPV.inc(la, 1)
                k.on("tensor", t_pv)

                def s_aoev(scalar, h=h, qc=qc, c=c):
                    sPV.wait(scalar, c + 1)
                    if qc == 0 and h >= 2:
                        sST.wait(scalar, 16 * (h - 1))
                    sAO.inc(scalar.activation(
                        ao[h % 2][:, qc * 512:qc * 512 + 512], pmix[:],
                        AF.Identity, bias=bvs[:, h:h + 1]), 1)
                k.on("scalar", s_aoev)
            if h + 2 < NHEAD:
                k.on("vector", mk_v_wc(h + 2))

        k.entry = [sST, sAO]


_NC_CACHE = {}


def _get_nc(upto=6, taps=()):
    key = (upto, tuple(taps))
    if key not in _NC_CACHE:
        _NC_CACHE[key] = build_program(upto, taps)
    return _NC_CACHE[key]


def make_inputs(ins, b):
    pi = _pi(H)
    rho = _pi(IFF)
    sig = np.concatenate([rho, IFF + rho])

    f32 = np.float32
    x = np.ascontiguousarray(ins["L0_attn_a_x"][b].T[pi]).astype(f32, copy=False)

    def t(a):
        return np.ascontiguousarray(np.asarray(a).T)

    w_in_s = np.ascontiguousarray(t(ins["L0_attn_a_W_in_states"])[pi])
    w_in_g = np.ascontiguousarray(t(ins["L0_attn_a_W_in_gates"])[pi])
    w_dt_T = t(ins["L0_attn_a_W_dt"])
    w_bc_T = np.ascontiguousarray(
        np.concatenate([t(ins["L0_attn_a_W_b"]), t(ins["L0_attn_a_W_c"])], axis=1))
    wdt_r = np.ascontiguousarray(
        w_dt_T.reshape(NDT, 128, RDT).transpose(1, 0, 2).reshape(128, NDT * RDT))
    wbc_r = np.ascontiguousarray(
        w_bc_T.reshape(NDT, 128, 2 * NST).transpose(1, 0, 2).reshape(128, NDT * 2 * NST))
    w_dtp_T = t(ins["L0_attn_a_W_dtproj"])
    w_out_T = np.ascontiguousarray(t(ins["L0_attn_a_W_out"])[:, pi])

    def rearr_cols(v, w):
        return np.ascontiguousarray(
            np.asarray(v).reshape(NDT, 128, w).transpose(1, 0, 2).reshape(128, NDT * w))

    conv_w_r = rearr_cols(ins["L0_attn_a_conv_w"], KCV)
    conv_b_r = rearr_cols(ins["L0_attn_a_conv_b"], 1)
    b_dtp_r = rearr_cols(ins["L0_attn_a_b_dtproj"], 1)
    a_log_r = rearr_cols(ins["L0_attn_a_A_log"], NST)
    d_r = rearr_cols(ins["L0_attn_a_D"], 1)

    wq_T = np.ascontiguousarray(t(ins["L0_attn_b_Wq"])[pi])
    wk_T = np.ascontiguousarray(t(ins["L0_attn_b_Wk"])[pi])
    wv_T = np.ascontiguousarray(t(ins["L0_attn_b_Wv"])[pi])
    wo_T = np.ascontiguousarray(t(ins["L0_attn_b_Wo"])[:, pi])

    def hcols(v):
        return np.ascontiguousarray(
            np.asarray(v).reshape(NHT, 128).T)

    bq_r = hcols(ins["L0_attn_b_bq"])
    bk_r = hcols(ins["L0_attn_b_bk"])
    bv_r = hcols(ins["L0_attn_b_bv"])
    bo_r = hcols(np.asarray(ins["L0_attn_b_bo"])[pi])

    gup = np.asarray(ins["L0_ffn_gate_up_packed"])
    dnp = np.asarray(ins["L0_ffn_down_packed"])
    gub = np.ascontiguousarray(gup.view(np.uint8).T[:, sig])
    dnb = np.ascontiguousarray(dnp.view(np.uint8).T[:, pi])
    gu_scl = np.ascontiguousarray(np.asarray(ins["L0_ffn_gate_up_scales"])[:, sig])
    dn_scl = np.ascontiguousarray(np.asarray(ins["L0_ffn_down_scales"])[:, pi])

    e_gu = np.zeros((NST, 8 * 128), dtype=np.float16)
    for tb in range(8):
        for p in range(128):
            e_gu[2 * tb + (p >= 64), tb * 128 + p] = 1
    e_dn = np.zeros((NIT, 22 * 128), dtype=np.float16)
    for tb in range(22):
        for p in range(128):
            e_dn[2 * tb + (p >= 64), tb * 128 + p] = 1

    e_sel = np.zeros((NST, NST * 128), dtype=BF)
    for r in range(NST):
        e_sel[r, r * 128:(r + 1) * 128] = 1

    mdiag = np.zeros((128, 4 * 512), dtype=np.float32)
    for r in range(4):
        for p in range(128):
            jj = np.arange(512)
            mdiag[p, r * 512:(r + 1) * 512] = np.where(jj >= 128 * r + p, 0.0, NEG)
    mdiag = mdiag.astype(BF)
    jj = np.arange(128)
    mdiag1 = np.where(jj[None, :] <= np.arange(128)[:, None], 0.0,
                      NEG).astype(np.float32)

    return dict(
        x_t=x, w_in_s=w_in_s, w_in_g=w_in_g, wdt_r=wdt_r, wbc_r=wbc_r,
        w_dtp_T=w_dtp_T, w_out_T=w_out_T, conv_w_r=conv_w_r, conv_b_r=conv_b_r,
        b_dtp_r=b_dtp_r, a_log_r=a_log_r, d_r=d_r,
        wq_T=wq_T, wk_T=wk_T, wv_T=wv_T, wo_T=wo_T,
        bq_r=bq_r, bk_r=bk_r, bv_r=bv_r, bo_r=bo_r,
        gub=gub, dnb=dnb, gu_scl=gu_scl, dn_scl=dn_scl,
        e_gu=e_gu, e_dn=e_dn,
        ones1=np.ones((1, 128), dtype=BF),
        onesc=np.ones((128, 1), dtype=BF),
        ident=np.eye(128, dtype=BF),
        identf=np.eye(128, dtype=np.float32),
        e_sel=e_sel, mdiag=mdiag, mdiag1=mdiag1,
        cfc=np.ascontiguousarray(
            np.broadcast_to(np.array(CFCV + [0.0] * (16 - len(CFCV)),
                                     dtype=np.float32), (128, 16))),
    )


def kernel(**inputs):
    nc = _get_nc(upto=6)
    pi = _pi(H)
    in_maps = [make_inputs(inputs, b) for b in range(8)]
    res = run_bass_kernel_spmd(nc, in_maps, core_ids=list(range(8)))
    out = np.empty((8, L, H), dtype=np.float32)
    for b in range(8):
        out[b][:, pi] = np.asarray(res.results[b]["out_t"]).T
    return out


if __name__ == "__main__":
    pass


def build_attn_only(taps=()):
    nc = bass.Bass("TRN2", target_bir_lowering=False, debug=False)

    def dram(name, shape, dt, kind="ExternalInput"):
        return nc.dram_tensor(name, shape, dt, kind=kind)
    xr2 = dram("xr2", [H, L], F32)
    wq_T = dram("wq_T", [H, H], F32)
    wk_T = dram("wk_T", [H, H], F32)
    wv_T = dram("wv_T", [H, H], F32)
    bq_r = dram("bq_r", [128, NHT], F32)
    bk_r = dram("bk_r", [128, NHT], F32)
    bv_r = dram("bv_r", [128, NHT], F32)
    ones1 = dram("ones1", [1, 128], BF16)
    onesc = dram("onesc", [128, 1], BF16)
    ident = dram("ident", [128, 128], BF16)
    identf = dram("identf", [128, 128], F32)
    e_sel = dram("e_sel", [NST, NST * 128], BF16)
    mdiag = dram("mdiag", [128, 4 * 512], BF16)
    mdiag1 = dram("mdiag1", [128, 128], F32)
    cfc = dram("cfc", [128, 16], F32)
    aout_s = nc.dram_tensor("aout_s", [H, L], BF16,
                            kind="ExternalOutput" if "aout_s" in taps else "Internal")
    with contextlib.ExitStack() as ctx:
        blk = ctx.enter_context(nc.Block())
        k = K(nc, blk, ctx)
        cE = ctx.enter_context
        cst = {
            "ones1": cE(nc.sbuf_tensor("c_ones1", [1, 128], BF16)),
            "onesc": cE(nc.sbuf_tensor("c_onesc", [128, 1], BF16)),
            "ident": cE(nc.sbuf_tensor("c_ident", [128, 128], BF16)),
            "identf": cE(nc.sbuf_tensor("c_identf", [128, 128], F32)),
            "cfc": cE(nc.sbuf_tensor("c_cfc", [128, 16], F32)),
        }
        for ci, v in enumerate(CFCV):
            nc.const_aps.register(F32, float(v), cst["cfc"][:, ci:ci + 1])
        sld = k.sem("sld")

        def load_consts(sync):
            for key, srcd in [("ones1", ones1), ("onesc", onesc),
                              ("ident", ident), ("identf", identf),
                              ("cfc", cfc)]:
                sld.dma(sync, cst[key][:], srcd.ap())
        k.on("sync", load_consts)
        k.entry = [sld]
        phase_attn(k, cst, xr2, wq_T, wk_T, wv_T, bq_r, bk_r, bv_r,
                   e_sel, mdiag, mdiag1, aout_s)
        fin = [(s, s.n) for s in k.sems.values() if s.n > 0]

        def final_wait(g, fin=fin):
            for s, v in fin:
                s.wait(g, v)
        k.on("gpsimd", final_wait)
        k.check()
    return nc
